# version 01
# baseline (speedup 1.0000x reference)
import numpy as np
from contextlib import ExitStack

import concourse.bass as bass
import concourse.mybir as mybir
from concourse.bass_utils import run_bass_kernel_spmd
from concourse.alu_op_type import AluOpType as OP

F32 = mybir.dt.float32
BF16 = mybir.dt.bfloat16
U8 = mybir.dt.uint8
U32 = mybir.dt.uint32
I32 = mybir.dt.int32
AF = mybir.ActivationFunctionType
AX = mybir.AxisListType

D = 1024
T = 4096
NDT = D // 128
TC = 512
NTC = T // TC
NEG_BIG = -1e30

ENGINES = ("sync", "vector", "scalar", "tensor", "gpsimd")
DMA_SEM = {"sync": "dS", "vector": "dV", "scalar": "dA", "gpsimd": "dG"}
SEM_NAMES = ["dS", "dV", "dA", "dG", "v", "a", "t", "p", "c"]


class P:

    def __init__(self, nc, stack=None):
        self.nc = nc
        self.stack = stack
        self._consts = {}
        self._cpool = {}
        self._cpool_n = {}
        if stack is not None:
            for dt_, tag in ((F32, "f"), (BF16, "b")):
                self._cpool[dt_] = stack.enter_context(
                    nc.sbuf_tensor(f"cpool_{tag}", [128, 32], dt_))
                self._cpool_n[dt_] = 0
        self.thunks = {e: [] for e in ENGINES}
        self.cnt = {s: 0 for s in SEM_NAMES}
        self.waited = {e: {s: 0 for s in SEM_NAMES} for e in ENGINES}
        self.sems = {}
        self.ready = []

    def set_ready(self, toks):
        best = {}
        for tok in toks:
            if tok is None:
                continue
            s, v = tok
            if v > best.get(s, -1):
                best[s] = v
        self.ready = [(s, v) for s, v in best.items()]

    def wait(self, e, toks):
        toks = list(toks) + self.ready
        for tok in toks:
            if tok is None:
                continue
            s, val = tok
            if val > self.waited[e][s]:
                self.waited[e][s] = val
                self.thunks[e].append(
                    lambda eng, s=s, val=val: eng.wait_ge(self.sems[s], val))

    def op(self, e, fn, wait=(), inc=None, inck=1):
        self.wait(e, wait)
        tok = None
        if inc is not None:
            self.cnt[inc] += inck
            tok = (inc, self.cnt[inc])

        def thunk(eng, fn=fn, inc=inc, inck=inck):
            instr = fn(eng)
            if inc is not None:
                instr.then_inc(self.sems[inc], inck)

        self.thunks[e].append(thunk)
        return tok

    def constp(self, val, dtype=F32):
        key = (float(val), dtype)
        if key in self._consts:
            return self._consts[key]
        i = self._cpool_n[dtype]
        assert i < 32, "const pool exhausted"
        self._cpool_n[dtype] = i + 1
        ap = self._cpool[dtype][:, i:i + 1]
        t0 = self.v(lambda e, t=ap, v=val: e.memset(t, v), inc="v")
        tok = self.v(lambda e: e.engine_nop(), wait=[t0], inc="v")
        self._consts[key] = (ap, tok)
        return self._consts[key]

    SERIAL_DMA = False

    def dma(self, q, out, in_, wait=()):
        s = DMA_SEM[q]
        if P.SERIAL_DMA:
            self.wait(q, [(s, self.cnt[s])])
        return self.op(q, lambda eng, o=out, i=in_: eng.dma_start(out=o, in_=i),
                       wait, inc=s, inck=16)

    def fence(self, q):
        s = DMA_SEM[q]
        self.wait(q, [(s, self.cnt[s])])
        return self.op(q, lambda eng: eng.nop(), inc=s, inck=16)

    def v(self, fn, wait=(), inc=None):
        return self.op("vector", fn, wait, inc)

    def s(self, fn, wait=(), inc=None):
        return self.op("scalar", fn, wait, inc)

    def t(self, fn, wait=(), inc=None):
        return self.op("tensor", fn, wait, inc)

    def g(self, fn, wait=(), inc=None):
        return self.op("gpsimd", fn, wait, inc)

    def build(self, stack):
        nc = self.nc
        for s in SEM_NAMES:
            self.sems[s] = stack.enter_context(nc.semaphore(s))
        block = stack.enter_context(nc.Block())
        thunks = self.thunks

        @block.sync
        def _(eng):
            for th in thunks["sync"]:
                th(eng)

        @block.vector
        def _(eng):
            for th in thunks["vector"]:
                th(eng)

        @block.scalar
        def _(eng):
            for th in thunks["scalar"]:
                th(eng)

        @block.tensor
        def _(eng):
            for th in thunks["tensor"]:
                th(eng)

        @block.gpsimd
        def _(eng):
            for th in thunks["gpsimd"]:
                th(eng)


class Rot:

    def __init__(self, tiles):
        self.tiles = tiles
        self.i = 0
        self.deps = [[] for _ in tiles]

    def next(self):
        i = self.i % len(self.tiles)
        self.i += 1
        w = self.deps[i]
        self.deps[i] = []
        return i, self.tiles[i], w

    def add_dep(self, i, tok):
        if tok is not None:
            self.deps[i].append(tok)


def ceil_div(a, b):
    return (a + b - 1) // b


def sbuf_rot(nc, st, nm, n, shape, dtype):
    return Rot([st.enter_context(nc.sbuf_tensor(f"{nm}{i}", shape, dtype))
                for i in range(n)])


def psum_rot(nc, st, nm, n, shape):
    return Rot([st.enter_context(nc.psum_tensor(f"{nm}{i}", shape, F32))
                for i in range(n)])




def emit_bcast(p, st, nm, row_ap, N, out_tile, out_col0, ready, pspool,
               ones_rb, ones_tok, nparts=128):
    nc = p.nc
    his = sbuf_rot(nc, st, nm + "_hi", 2, [1, TC], BF16)
    lo32s = sbuf_rot(nc, st, nm + "_l3", 2, [1, TC], F32)
    los = sbuf_rot(nc, st, nm + "_lo", 2, [1, TC], BF16)
    toks = []
    for c0 in range(0, N, TC):
        w = min(TC, N - c0)
        rap = row_ap[0:1, c0:c0 + w]
        hii, hi, hid = his.next()
        l3i, lo32, l3d = lo32s.next()
        loi, lo, lod = los.next()
        h1 = p.v(lambda e, hi=hi, rap=rap, w=w: e.tensor_copy(hi[0:1, 0:w], rap),
                 wait=list(ready) + hid, inc="v")
        h2 = p.v(lambda e, lo32=lo32, rap=rap, hi=hi, w=w:
                 e.tensor_sub(lo32[0:1, 0:w], rap, hi[0:1, 0:w]),
                 wait=[h1] + l3d, inc="v")
        h3 = p.v(lambda e, lo=lo, lo32=lo32, w=w:
                 e.tensor_copy(lo[0:1, 0:w], lo32[0:1, 0:w]),
                 wait=[h2] + lod, inc="v")
        lo32s.add_dep(l3i, h3)
        pi, ps, pdeps = pspool.next()
        p.t(lambda e, ps=ps, hi=hi, w=w: e.matmul(
            ps[0:nparts, 0:w], ones_rb[0:1, 0:nparts], hi[0:1, 0:w],
            start=True, stop=False), wait=pdeps + [h3, ones_tok])
        mt = p.t(lambda e, ps=ps, lo=lo, w=w: e.matmul(
            ps[0:nparts, 0:w], ones_rb[0:1, 0:nparts], lo[0:1, 0:w],
            start=False, stop=True), inc="t")
        his.add_dep(hii, mt)
        los.add_dep(loi, mt)
        et = p.v(lambda e, ps=ps, c0=c0, w=w: e.tensor_copy(
            out_tile[0:nparts, out_col0 + c0:out_col0 + c0 + w], ps[0:nparts, 0:w]),
            wait=[mt], inc="v")
        pspool.add_dep(pi, et)
        toks.append(et)
    return toks


def emit_mm(p, st, nm, out_specs, pairs, M, N,
            act=None, scale_pp=None, res_dram=(),
            rhs_cache=None, extra_evac=None):
    nc = p.nc
    n_mt = ceil_div(M, 128)
    n_nc = ceil_div(N, TC)

    rhs_tiles = {}
    need_load = False
    for j, (lT, ldt, rhs, K) in enumerate(pairs):
        if rhs is None:
            continue
        key = id(rhs.tensor)
        if key in rhs_tiles:
            continue
        if rhs_cache is not None and key in rhs_cache:
            rhs_tiles[key] = rhs_cache[key]
            continue
        nk = ceil_div(K, 128)
        tiles = []
        for kk in range(nk):
            kp = min(128, K - kk * 128)
            tile = st.enter_context(
                nc.sbuf_tensor(f"{nm}_rhs{j}_{kk}", [kp, N], BF16))
            p.dma("sync", tile[:, :], rhs[kk * 128:kk * 128 + kp, :])
            tiles.append((tile, kp))
            need_load = True
        rhs_tiles[key] = tiles
    rhs_tok = p.fence("sync") if need_load else None
    if rhs_cache is not None:
        for j, (lT, ldt, rhs, K) in enumerate(pairs):
            if rhs is not None:
                rhs_cache[id(rhs.tensor)] = rhs_tiles[id(rhs.tensor)]

    ones_tok = None
    ones_row = None
    if any(r is None for (_, _, r, _) in pairs):
        ones_row = st.enter_context(nc.sbuf_tensor(f"{nm}_ones", [1, N], BF16))
        t0 = p.v(lambda e: e.memset(ones_row[:, :], 1.0), inc="v")
        ones_tok = p.v(lambda e: e.engine_nop(), wait=[t0], inc="v")

    total_chunks = sum(ceil_div(K, 128) for (_, _, _, K) in pairs)
    wf = sbuf_rot(nc, st, f"{nm}_wf", total_chunks, [128, 128], F32)
    wb = sbuf_rot(nc, st, f"{nm}_wb", total_chunks, [128, 128], BF16)
    pss = psum_rot(nc, st, f"{nm}_ps", 2, [128, TC])
    scr = sbuf_rot(nc, st, f"{nm}_sc", 6, [128, TC], F32)
    res_bufs = [sbuf_rot(nc, st, f"{nm}_rb{k}", n_nc, [128, TC], F32)
                for k in range(len(res_dram))]
    out_bufs = [sbuf_rot(nc, st, f"{nm}_ob{oi}", n_nc, [128, TC], odt)
                for oi, (_, odt) in enumerate(out_specs)]
    out_war = [None]

    out_toks = []
    for mt in range(n_mt):
        mp = min(128, M - mt * 128)
        wdmas = []
        for j, (lT, ldt, rhs, K) in enumerate(pairs):
            nk = ceil_div(K, 128)
            for kk in range(nk):
                kp = min(128, K - kk * 128)
                src = lT[kk * 128:kk * 128 + kp, mt * 128:mt * 128 + mp]
                if ldt == F32:
                    fi, ftile, fdeps = wf.next()
                    p.dma("sync", ftile[0:kp, 0:mp], src, wait=fdeps)
                    wdmas.append(("f", j, kk, kp, fi, ftile))
                else:
                    bi, btile, bdeps = wb.next()
                    p.dma("sync", btile[0:kp, 0:mp], src, wait=bdeps)
                    wdmas.append(("b", j, kk, kp, bi, btile))
        wfence = p.fence("sync")
        wchunks = {}
        for (kind, j, kk, kp, i_, tile) in wdmas:
            if kind == "f":
                bi, btile, bdeps = wb.next()
                ctok = p.v(lambda e, b=btile, f=tile, kp=kp, mp=mp:
                           e.tensor_copy(b[0:kp, 0:mp], f[0:kp, 0:mp]),
                           wait=bdeps + [wfence], inc="v")
                wf.add_dep(i_, ctok)
                wchunks[(j, kk)] = (btile, bi, ctok, kp)
            else:
                wchunks[(j, kk)] = (tile, i_, wfence, kp)
        res_slots = []
        for k, rd in enumerate(res_dram):
            for ncc in range(n_nc):
                np_ = min(TC, N - ncc * TC)
                ri2, rtile, rdeps2 = res_bufs[k].next()
                p.dma("scalar", rtile[0:mp, 0:np_],
                      rd[mt * 128:mt * 128 + mp, ncc * TC:ncc * TC + np_],
                      wait=rdeps2)
                res_slots.append((k, ncc, ri2, rtile))
        rfence = p.fence("scalar") if res_dram else None

        for ncc in range(n_nc):
            np_ = min(TC, N - ncc * TC)
            pi, ps, pdeps = pss.next()
            mm_wait = list(pdeps)
            cnt = 0
            mm_tok = None
            for j, (lT, ldt, rhs, K) in enumerate(pairs):
                nk = ceil_div(K, 128)
                for kk in range(nk):
                    btile, bi, ctok, kp = wchunks[(j, kk)]
                    cnt += 1
                    if rhs is None:
                        rap = ones_row[0:1, ncc * TC:ncc * TC + np_]
                        rtok = ones_tok
                    else:
                        rtile, rkp = rhs_tiles[id(rhs.tensor)][kk]
                        rap = rtile[0:rkp, ncc * TC:ncc * TC + np_]
                        rtok = rhs_tok
                    w = mm_wait + [ctok, rtok]
                    mm_wait = []
                    last = (cnt == total_chunks)
                    mm_tok = p.t(lambda e, ps=ps, b=btile, kp=kp, mp=mp, rap=rap,
                                 first=(cnt == 1), last=last, np_=np_:
                                 e.matmul(ps[0:mp, 0:np_], b[0:kp, 0:mp], rap,
                                          start=first, stop=last),
                                 wait=w, inc="t" if last else None)
            for key2 in wchunks:
                wb.add_dep(wchunks[key2][1], mm_tok)
            state = {"ap": None, "tok": None, "rot": None, "ri": None}

            def put(fn, extra_wait=(), engine="vector", rot=scr, _state=state,
                    mp=mp, np_=np_):
                ri, tile, rdeps = rot.next()
                o_ap = tile[0:mp, 0:np_]
                w = list(rdeps) + list(extra_wait)
                if _state["tok"] is not None:
                    w.append(_state["tok"])
                in_ap = _state["ap"]
                tok = p.op(engine, lambda e, o=o_ap, i=in_ap, fn=fn: fn(e, o, i),
                           wait=w, inc="v" if engine == "vector" else "a")
                if _state["rot"] is not None:
                    _state["rot"].add_dep(_state["ri"], tok)
                _state.update(ap=o_ap, tok=tok, rot=rot, ri=ri)
                return tok

            state["ap"] = ps[0:mp, 0:np_]
            put(lambda e, o, i, act=act: e.activation(o, i, act if act else AF.Copy),
                extra_wait=[mm_tok], engine="scalar")
            pss.add_dep(pi, state["tok"])
            if scale_pp is not None:
                spp, sptok = scale_pp(mt)
                put(lambda e, o, i, spp=spp: e.tensor_scalar_mul(o, i, spp),
                    extra_wait=[sptok])
            for k, rd in enumerate(res_dram):
                ri2 = None
                for (k2, ncc2, rid, rtile) in res_slots:
                    if k2 == k and ncc2 == ncc:
                        ri2, rt_ = rid, rtile
                rap2 = rt_[0:mp, 0:np_]
                put(lambda e, o, i, r=rap2: e.tensor_add(o, i, r),
                    extra_wait=[rfence])
                res_bufs[k].add_dep(ri2, state["tok"])
            if extra_evac is not None:
                extra_evac(p, mt, ncc, state, put)
            for oi, (odr, odt) in enumerate(out_specs):
                bi, btile, bdeps = out_bufs[oi].next()
                ktok = p.v(lambda e, o=btile[0:mp, 0:np_], i=state["ap"]:
                           e.tensor_copy(o, i),
                           wait=bdeps + [state["tok"], out_war[0]], inc="v")
                state["rot"].add_dep(state["ri"], ktok)
                wtok = p.dma("gpsimd",
                             odr[mt * 128:mt * 128 + mp, ncc * TC:ncc * TC + np_],
                             btile[0:mp, 0:np_], wait=[ktok])
                out_toks.append(wtok)
        out_war[0] = p.fence("gpsimd")
    out_toks.append(out_war[0])
    return out_toks



def emit_ln(p, st, nm, x_dram, x_ready, outs, gamma_beta=None, eps=1e-5, dbg=None):
    nc = p.nc
    ones_c = st.enter_context(nc.sbuf_tensor(nm + "_onc", [128, 1], F32))
    o1 = p.v(lambda e: e.memset(ones_c[:, :], 1.0), inc="v")
    o3 = p.v(lambda e: e.engine_nop(), wait=[o1], inc="v")
    xbufs = sbuf_rot(nc, st, nm + "_x", NDT, [128, TC], F32)
    sqbufs = sbuf_rot(nc, st, nm + "_q", NDT, [128, TC], F32)
    ps_sum = st.enter_context(nc.psum_tensor(nm + "_pss", [1, TC], F32))
    ps_sq = st.enter_context(nc.psum_tensor(nm + "_psq", [1, TC], F32))
    sums = st.enter_context(nc.sbuf_tensor(nm + "_sums", [1, T], F32))
    sqs = st.enter_context(nc.sbuf_tensor(nm + "_sqs", [1, T], F32))
    rrow = st.enter_context(nc.sbuf_tensor(nm + "_rr", [1, T], F32))
    mrrow = st.enter_context(nc.sbuf_tensor(nm + "_mr", [1, T], F32))
    srow = st.enter_context(nc.sbuf_tensor(nm + "_sr", [1, T], F32))
    rb = st.enter_context(nc.sbuf_tensor(nm + "_rb", [128, T], F32))
    mrb = st.enter_context(nc.sbuf_tensor(nm + "_mrb", [128, T], F32))
    ps_b = psum_rot(nc, st, nm + "_pb", 2, [128, TC])

    evtoks = []
    for ncc in range(NTC):
        ps_deps = evtoks[-2:]
        chunks = []
        for dt in range(NDT):
            xi, xt, xdeps = xbufs.next()
            p.dma("sync", xt[:, :],
                  x_dram[dt * 128:(dt + 1) * 128, ncc * TC:(ncc + 1) * TC],
                  wait=xdeps + list(x_ready))
            chunks.append([xi, xt, None, None, None])
        xf = p.fence("sync")
        for dt, ch in enumerate(chunks):
            qi, qt, qdeps = sqbufs.next()
            qtok = p.s(lambda e, q=qt, x=ch[1]: e.activation(q[:, :], x[:, :],
                                                             AF.Square),
                       wait=qdeps + [xf], inc="a")
            ch[2], ch[3], ch[4] = qi, qt, qtok
        tA = None
        for dt, (xi, xt, qi, qt, qtok) in enumerate(chunks):
            tA = p.t(lambda e, x=xt, f=(dt == 0), l=(dt == NDT - 1):
                     e.matmul(ps_sum[:, :], ones_c[:, :], x[:, :], start=f, stop=l),
                     wait=[o3, xf] + (ps_deps if dt == 0 else []),
                     inc="t" if dt == NDT - 1 else None)
        mmtok = None
        for dt, (xi, xt, qi, qt, qtok) in enumerate(chunks):
            mmtok = p.t(lambda e, q=qt, f=(dt == 0), l=(dt == NDT - 1):
                        e.matmul(ps_sq[:, :], ones_c[:, :], q[:, :], start=f, stop=l),
                        wait=[qtok] + (ps_deps if dt == 0 else []),
                        inc="t" if dt == NDT - 1 else None)
        for (xi, xt, qi, qt, qtok) in chunks:
            xbufs.add_dep(xi, tA)
            sqbufs.add_dep(qi, mmtok)
        e1 = p.v(lambda e, ncc=ncc: e.tensor_copy(
            sums[0:1, ncc * TC:(ncc + 1) * TC], ps_sum[:, :]), wait=[mmtok], inc="v")
        e2 = p.v(lambda e, ncc=ncc: e.tensor_copy(
            sqs[0:1, ncc * TC:(ncc + 1) * TC], ps_sq[:, :]), wait=[mmtok], inc="v")
        evtoks += [e1, e2]
    s1 = p.v(lambda e: e.tensor_scalar_mul(sums[:, :], sums[:, :], 1.0 / D),
             wait=evtoks, inc="v")
    s2 = p.v(lambda e: e.tensor_scalar_mul(sqs[:, :], sqs[:, :], 1.0 / D),
             wait=evtoks, inc="v")
    s3 = p.v(lambda e: e.tensor_mul(srow[:, :], sums[:, :], sums[:, :]),
             wait=[s1], inc="v")
    s4 = p.v(lambda e: e.tensor_sub(sqs[:, :], sqs[:, :], srow[:, :]),
             wait=[s2, s3], inc="v")
    epsap, epstok = p.constp(eps)
    negap, negtok = p.constp(-1.0)
    s5 = p.s(lambda e: e.activation(srow[:, :], sqs[:, :], AF.Sqrt,
                                    bias=epsap[0:1, 0:1]),
             wait=[s4, epstok], inc="a")
    s6 = p.s(lambda e: e.activation(rrow[:, :], srow[:, :], AF.Ln),
             wait=[s5], inc="a")
    s7 = p.s(lambda e: e.activation(rrow[:, :], rrow[:, :], AF.Exp,
                                    scale=negap[0:1, 0:1]),
             wait=[s6, negtok], inc="a")
    s8 = p.v(lambda e: e.tensor_mul(mrrow[:, :], sums[:, :], rrow[:, :]),
             wait=[s7, s1], inc="v")
    ones_rb = st.enter_context(nc.sbuf_tensor(nm + "_onb", [1, 128], BF16))
    ob1 = p.v(lambda e: e.memset(ones_rb[:, :], 1.0), inc="v")
    ob2 = p.v(lambda e: e.engine_nop(), wait=[ob1], inc="v")
    btoks = []
    btoks += emit_bcast(p, st, nm + "_br", rrow, T, rb, 0,
                        [s7, s8, ob2], ps_b, ones_rb, ob2)
    btoks += emit_bcast(p, st, nm + "_bm", mrrow, T, mrb, 0,
                        [s7, s8, ob2], ps_b, ones_rb, ob2)
    obufs = [sbuf_rot(nc, st, f"{nm}_o{oi}", NTC, [128, TC], odt)
             for oi, (_, odt) in enumerate(outs)]
    scr = sbuf_rot(nc, st, nm + "_c", 4, [128, TC], F32)
    out_toks = []
    if dbg is not None:
        p.dma("gpsimd", dbg[0:1, :], sums[0:1, :], wait=[s8] + btoks)
        p.dma("gpsimd", dbg[1:2, :], sqs[0:1, :], wait=[s8] + btoks)
        p.dma("gpsimd", dbg[2:3, :], rrow[0:1, :], wait=[s8] + btoks)
        p.dma("gpsimd", dbg[3:4, :], mrrow[0:1, :], wait=[s8] + btoks)
    owar = None
    for dt in range(NDT):
        loads = []
        for ncc in range(NTC):
            xi, xt, xdeps = xbufs.next()
            p.dma("sync", xt[:, :],
                  x_dram[dt * 128:(dt + 1) * 128, ncc * TC:(ncc + 1) * TC],
                  wait=xdeps)
            loads.append((xi, xt))
        xf = p.fence("sync")
        for ncc, (xi, xt) in enumerate(loads):
            ci, ct, cdeps = scr.next()
            m1 = p.v(lambda e, c=ct, x=xt, ncc=ncc:
                     e.tensor_mul(c[:, :], x[:, :], rb[:, ncc * TC:(ncc + 1) * TC]),
                     wait=cdeps + [xf] + btoks, inc="v")
            xbufs.add_dep(xi, m1)
            m2 = p.v(lambda e, c=ct, ncc=ncc:
                     e.tensor_sub(c[:, :], c[:, :], mrb[:, ncc * TC:(ncc + 1) * TC]),
                     wait=[m1], inc="v")
            cur_tok = m2
            if gamma_beta is not None:
                gfn, bfn = gamma_beta
                gap, gtok = gfn(dt)
                bap, btok2 = bfn(dt)
                cur_tok = p.v(lambda e, c=ct, g=gap, b=bap:
                              e.tensor_scalar(c[:, :], c[:, :], g, b, OP.mult, OP.add),
                              wait=[m2, gtok, btok2], inc="v")
            for oi, (odr, odt) in enumerate(outs):
                bi, bt, bdeps = obufs[oi].next()
                ktok = p.v(lambda e, o=bt, c=ct: e.tensor_copy(o[:, :], c[:, :]),
                           wait=bdeps + [cur_tok, owar], inc="v")
                scr.add_dep(ci, ktok)
                wtok = p.dma("gpsimd",
                             odr[dt * 128:(dt + 1) * 128, ncc * TC:(ncc + 1) * TC],
                             bt[:, :], wait=[ktok])
                out_toks.append(wtok)
        owar = p.fence("gpsimd")
    out_toks.append(owar)
    return out_toks



def emit_rwkv(p, st, nm, x_dram, xnew_dram, W, scr_dram):
    nc = p.nc
    xp_dr = scr_dram(nm + "_xp", [D, T], BF16)
    E_dr = scr_dram(nm + "_E", [D, T], F32)
    v_dr = scr_dram(nm + "_v", [D, T], F32)
    r_dr = scr_dram(nm + "_r", [D, T], F32)
    y_dr = scr_dram(nm + "_y", [D, T], BF16)

    with ExitStack() as s0:
        ln_toks = emit_ln(p, s0, nm + "_ln", x_dram, (), [(xp_dr, BF16)])
    p.set_ready(ln_toks)

    with ExitStack() as s1:
        tmt = {}
        for name in ("tmk", "tmv", "tmr"):
            tile = s1.enter_context(nc.sbuf_tensor(f"{nm}_{name}", [128, NDT], F32))
            for dt in range(NDT):
                p.dma("scalar", tile[:, dt:dt + 1],
                      W[name][dt * 128:(dt + 1) * 128, 0:1])
            tmt[name] = tile
        cfence = p.fence("scalar")
        wres = {w: s1.enter_context(nc.sbuf_tensor(f"{nm}_{w}", [128, 64 * 128], BF16))
                for w in ("WkT", "WvT", "WrT")}
        wstage = s1.enter_context(nc.sbuf_tensor(nm + "_wst", [128, 64 * 128], F32))
        wready = []
        prev_cv = None
        for wname in ("WkT", "WvT", "WrT"):
            for cc in range(NDT):
                p.dma("sync", wstage[:, cc * 1024:(cc + 1) * 1024],
                      W[wname][cc * 128:(cc + 1) * 128, :],
                      wait=[prev_cv] if prev_cv else [])
            wfc = p.fence("sync")
            cts = []
            for cc in range(NDT):
                for dt in range(NDT):
                    ct = p.v(lambda e, o=wres[wname][:, (cc * 8 + dt) * 128:
                                                     (cc * 8 + dt + 1) * 128],
                             i=wstage[:, cc * 1024 + dt * 128:
                                      cc * 1024 + (dt + 1) * 128]:
                             e.tensor_copy(o, i), wait=[wfc], inc="v")
                    cts.append(ct)
            prev_cv = p.v(lambda e: e.engine_nop(), wait=cts[-1:], inc="v")
            wready.append(prev_cv)

        xw = sbuf_rot(nc, s1, nm + "_xw", NDT, [128, TC + 1], BF16)
        xm = sbuf_rot(nc, s1, nm + "_xm", NDT * 2, [128, TC], BF16)
        dsc = sbuf_rot(nc, s1, nm + "_dsc", 4, [128, TC], BF16)
        pss = psum_rot(nc, s1, nm + "_ps", 2, [128, TC])
        stg = sbuf_rot(nc, s1, nm + "_st", NDT, [128, TC], F32)
        stream_par = [("k", "WkT", "tmk", AF.Exp, E_dr),
                      ("v", "WvT", "tmv", None, v_dr),
                      ("r", "WrT", "tmr", AF.Sigmoid, r_dr)]
        for tc in range(NTC):
            wins = []
            for dt in range(NDT):
                xi, xt, xdeps = xw.next()
                if tc == 0:
                    p.dma("sync", xt[:, 1:TC + 1],
                          xp_dr[dt * 128:(dt + 1) * 128, 0:TC], wait=xdeps)
                else:
                    p.dma("sync", xt[:, :],
                          xp_dr[dt * 128:(dt + 1) * 128, tc * TC - 1:tc * TC + TC],
                          wait=xdeps)
                wins.append((xi, xt))
            xf = p.fence("sync")
            if tc == 0:
                ztoks = []
                for (xi, xt) in wins:
                    ztoks.append(p.v(lambda e, x=xt: e.memset(x[:, 0:1], 0.0),
                                     wait=[xf], inc="v"))
                xf = p.v(lambda e: e.engine_nop(), wait=ztoks, inc="v")
            for sname, wname, tmname, actf, odr in stream_par:
                xmtiles = []
                for dt, (xi, xt) in enumerate(wins):
                    di, dtl, ddeps = dsc.next()
                    d1 = p.v(lambda e, o=dtl, x=xt:
                             e.tensor_sub(o[:, :], x[:, 1:TC + 1], x[:, 0:TC]),
                             wait=ddeps + [xf, cfence], inc="v")
                    mi, mtl, mdeps = xm.next()
                    m1 = p.v(lambda e, o=mtl, dd=dtl, x=xt,
                             tm=tmt[tmname][:, dt:dt + 1]:
                             e.scalar_tensor_tensor(o[:, :], dd[:, :], tm, x[:, 0:TC],
                                                    OP.mult, OP.add),
                             wait=mdeps + [d1], inc="v")
                    dsc.add_dep(di, m1)
                    xmtiles.append((mi, mtl, m1))
                sttoks = []
                for dt in range(NDT):
                    pi, ps, pdeps = pss.next()
                    mm_tok = None
                    for cc in range(NDT):
                        wap = wres[wname][:, (cc * 8 + dt) * 128:
                                          (cc * 8 + dt + 1) * 128]
                        mi, mtl, m1 = xmtiles[cc]
                        mm_tok = p.t(lambda e, ps=ps, w=wap, m=mtl,
                                     f=(cc == 0), l=(cc == NDT - 1):
                                     e.matmul(ps[:, :], w, m[:, :], start=f, stop=l),
                                     wait=(pdeps if cc == 0 else []) + [m1] +
                                          (wready if cc == 0 else []),
                                     inc="t" if cc == NDT - 1 else None)
                    for (mi, mtl, m1) in xmtiles:
                        xm.add_dep(mi, mm_tok)
                    si, stile, sdeps = stg.next()
                    if actf is not None:
                        ev = p.s(lambda e, s=stile, ps=ps, a=actf:
                                 e.activation(s[:, :], ps[:, :], a),
                                 wait=sdeps + [mm_tok], inc="a")
                    else:
                        ev = p.s(lambda e, s=stile, ps=ps: e.copy(s[:, :], ps[:, :]),
                                 wait=sdeps + [mm_tok], inc="a")
                    pss.add_dep(pi, ev)
                    p.dma("gpsimd", odr[dt * 128:(dt + 1) * 128,
                                        tc * TC:(tc + 1) * TC],
                          stile[:, :], wait=[ev])
                    sttoks.append(si)
                sfence = p.fence("gpsimd")
                for si in sttoks:
                    stg.add_dep(si, sfence)
                for dt, (xi, xt) in enumerate(wins):
                    xw.add_dep(xi, ("v", p.cnt["v"]))
        st_fence = p.fence("gpsimd")
    p.set_ready([st_fence])

    with ExitStack() as s2:
        Et = s2.enter_context(nc.sbuf_tensor(nm + "_Et", [128, T], F32))
        vt = s2.enter_context(nc.sbuf_tensor(nm + "_vt", [128, T], F32))
        rt = s2.enter_context(nc.sbuf_tensor(nm + "_rt", [128, T], F32))
        t1 = s2.enter_context(nc.sbuf_tensor(nm + "_t1", [128, T], F32))
        t2 = s2.enter_context(nc.sbuf_tensor(nm + "_t2", [128, T], F32))
        ab = s2.enter_context(nc.sbuf_tensor(nm + "_ab", [128, T + 1], F32))
        pb = s2.enter_context(nc.sbuf_tensor(nm + "_pb", [128, T + 1], F32))
        yst = s2.enter_context(nc.sbuf_tensor(nm + "_yst", [128, T], BF16))
        lam = s2.enter_context(nc.sbuf_tensor(nm + "_lam", [128, 2], F32))
        uu = s2.enter_context(nc.sbuf_tensor(nm + "_uu", [128, 2], F32))
        negap, negtok = p.constp(-1.0)
        prev = [negtok]
        for dt in range(NDT):
            p.dma("scalar", lam[:, 0:1], W["decay"][dt * 128:(dt + 1) * 128, 0:1],
                  wait=prev)
            p.dma("scalar", uu[:, 0:1], W["first"][dt * 128:(dt + 1) * 128, 0:1],
                  wait=prev)
            p.dma("sync", Et[:, :], E_dr[dt * 128:(dt + 1) * 128, :], wait=prev)
            p.dma("sync", vt[:, :], v_dr[dt * 128:(dt + 1) * 128, :], wait=prev)
            p.dma("sync", rt[:, :], r_dr[dt * 128:(dt + 1) * 128, :], wait=prev)
            cf = p.fence("scalar")
            df = p.fence("sync")
            l1 = p.s(lambda e: e.activation(lam[:, 1:2], lam[:, 0:1], AF.Exp),
                     wait=[cf], inc="a")
            u1 = p.s(lambda e: e.activation(uu[:, 1:2], uu[:, 0:1], AF.Exp),
                     wait=[l1], inc="a")
            z1 = p.v(lambda e: e.memset(ab[:, 0:1], 0.0), wait=prev, inc="v")
            z2 = p.v(lambda e: e.memset(pb[:, 0:1], 0.0), wait=[z1], inc="v")
            z3 = p.v(lambda e: e.engine_nop(), wait=[z2], inc="v")
            c1 = p.v(lambda e: e.tensor_mul(t1[:, :], Et[:, :], vt[:, :]),
                     wait=[df, z3], inc="v")
            lamb = lam[:, 1:2].broadcast_to([128, T])
            sc1 = p.v(lambda e, lb=lamb: e.tensor_tensor_scan(
                ab[:, 1:T + 1], lb, t1[:, :], 0.0, OP.mult, OP.add),
                wait=[c1, l1], inc="v")
            sc1b = p.v(lambda e: e.engine_nop(), wait=[sc1], inc="v")
            sc2 = p.v(lambda e, lb=lamb: e.tensor_tensor_scan(
                pb[:, 1:T + 1], lb, Et[:, :], 0.0, OP.mult, OP.max),
                wait=[sc1b], inc="v")
            sc2b = p.v(lambda e: e.engine_nop(), wait=[sc2], inc="v")
            n1 = p.v(lambda e: e.scalar_tensor_tensor(
                vt[:, :], t1[:, :], uu[:, 1:2], ab[:, 0:T], OP.mult, OP.add),
                wait=[sc2b, u1], inc="v")
            n2 = p.v(lambda e: e.tensor_scalar_mul(t1[:, :], Et[:, :], uu[:, 1:2]),
                     wait=[n1], inc="v")
            n3 = p.v(lambda e: e.tensor_add(t2[:, :], pb[:, 0:T], t1[:, :]),
                     wait=[n2], inc="v")
            n4 = p.v(lambda e: e.tensor_max(Et[:, :], pb[:, 0:T], t1[:, :]),
                     wait=[n3], inc="v")
            n5 = p.v(lambda e: e.scalar_tensor_tensor(
                ab[:, 0:T], Et[:, :], 1e-9, t2[:, :], OP.mult, OP.add),
                wait=[n4], inc="v")
            n6 = p.s(lambda e: e.activation(Et[:, :], ab[:, 0:T], AF.Ln),
                     wait=[n5], inc="a")
            n7 = p.s(lambda e, na=negap: e.activation(t2[:, :], Et[:, :], AF.Exp,
                                                      scale=na),
                     wait=[n6], inc="a")
            n8 = p.v(lambda e: e.tensor_mul(t1[:, :], vt[:, :], t2[:, :]),
                     wait=[n7], inc="v")
            n9 = p.v(lambda e: e.tensor_mul(yst[:, :], t1[:, :], rt[:, :]),
                     wait=[n8], inc="v")
            p.dma("gpsimd", y_dr[dt * 128:(dt + 1) * 128, :], yst[:, :],
                  wait=[n9])
            yfence = p.fence("gpsimd")
            prev = [yfence, n9]
    p.set_ready(prev)

    with ExitStack() as s3:
        toks = emit_mm(p, s3, nm + "_o", [(xnew_dram, F32)],
                       [(W["WoT"], F32, y_dr, D)], D, T, res_dram=(x_dram,))
    p.set_ready(toks)
    return toks



def emit_ttt(p, st, nm, x_dram, xnew_dram, W, scr_dram):
    nc = p.nc
    xp_bf = scr_dram(nm + "_xpb", [D, T], BF16)
    xp_f = scr_dram(nm + "_xpf", [D, T], F32)
    h_bf = scr_dram(nm + "_h", [D, T], BF16)
    Q_dr = scr_dram(nm + "_Q", [D, T], F32)
    Kb_dr = scr_dram(nm + "_Kb", [D, T], BF16)
    Kf_dr = scr_dram(nm + "_Kf", [D, T], F32)
    V_dr = scr_dram(nm + "_V", [D, T], F32)
    KA_dr = scr_dram(nm + "_KA", [8, T], BF16)
    y0_dr = scr_dram(nm + "_y0", [D, T], F32)
    gate_dr = scr_dram(nm + "_g", [D, T], F32)
    ob_dr = scr_dram(nm + "_ob", [D, T], BF16)

    par = st.enter_context(nc.sbuf_tensor(nm + "_par", [128, 4 * NDT], F32))
    for i, w in enumerate(("ln_g", "ln_b", "log_lr", "ls_g")):
        for dt in range(NDT):
            p.dma("scalar", par[:, i * NDT + dt:i * NDT + dt + 1],
                  W[w][dt * 128:(dt + 1) * 128, 0:1])
    parf = p.fence("scalar")
    lr_t = st.enter_context(nc.sbuf_tensor(nm + "_lr", [128, NDT], F32))
    lr1 = p.s(lambda e: e.activation(lr_t[:, :], par[:, 2 * NDT:3 * NDT], AF.Exp),
              wait=[parf], inc="a")
    lr2 = p.v(lambda e: e.tensor_scalar_min(lr_t[:, :], lr_t[:, :], 1.0),
              wait=[lr1], inc="v")

    with ExitStack() as s0:
        t0 = emit_ln(p, s0, nm + "_l1", x_dram, (), [(xp_bf, BF16), (xp_f, F32)])
    p.set_ready(t0)
    with ExitStack() as s0:
        t1_ = emit_ln(p, s0, nm + "_l2", xp_f, (),[(h_bf, BF16)],
                      gamma_beta=(lambda dt: (par[:, dt:dt + 1], parf),
                                  lambda dt: (par[:, NDT + dt:NDT + dt + 1], parf)))
    p.set_ready(t1_)
    with ExitStack() as s0:
        cache = {}
        tq = emit_mm(p, s0, nm + "_q", [(Q_dr, F32)], [(W["WqT"], F32, h_bf, D)],
                     D, T, rhs_cache=cache)
        tk = emit_mm(p, s0, nm + "_k", [(Kb_dr, BF16), (Kf_dr, F32)],
                     [(W["WkT"], F32, h_bf, D)], D, T, rhs_cache=cache)
        tv = emit_mm(p, s0, nm + "_v", [(V_dr, F32)], [(W["WvT"], F32, h_bf, D)],
                     D, T, rhs_cache=cache)
    p.set_ready(tq + tk + tv)
    with ExitStack() as s0:
        tka = emit_mm(p, s0, nm + "_ka", [(KA_dr, BF16)], [(W["A"], F32, Kb_dr, D)],
                      8, T)
    p.set_ready(tka)
    with ExitStack() as s0:
        ty0 = emit_mm(p, s0, nm + "_y0", [(y0_dr, F32)],
                      [(W["W0"], F32, Kb_dr, D), (W["Bm"], F32, KA_dr, 8)], D, T)
        tg = emit_mm(p, s0, nm + "_gm", [(gate_dr, F32)],
                     [(W["WgT"], F32, xp_bf, D)], D, T, act=AF.Sigmoid)
    p.set_ready(ty0 + tg)

    with ExitStack() as s2:
        y0t = s2.enter_context(nc.sbuf_tensor(nm + "_y0t", [128, T], F32))
        Vt = s2.enter_context(nc.sbuf_tensor(nm + "_Vt", [128, T], F32))
        Kt = s2.enter_context(nc.sbuf_tensor(nm + "_Kt", [128, T], F32))
        Qt = s2.enter_context(nc.sbuf_tensor(nm + "_Qt", [128, T], F32))
        gt = s2.enter_context(nc.sbuf_tensor(nm + "_gt", [128, T], F32))
        t1 = s2.enter_context(nc.sbuf_tensor(nm + "_t1", [128, T], F32))
        t2 = s2.enter_context(nc.sbuf_tensor(nm + "_t2", [128, T], F32))
        cs = s2.enter_context(nc.sbuf_tensor(nm + "_cs", [128, T + 1], F32))
        posb = s2.enter_context(nc.sbuf_tensor(nm + "_pb", [128, T], F32))
        posr = s2.enter_context(nc.sbuf_tensor(nm + "_pr", [1, T], F32))
        yst = s2.enter_context(nc.sbuf_tensor(nm + "_yst", [128, T], BF16))
        ones_rb = s2.enter_context(nc.sbuf_tensor(nm + "_onb", [1, 128], BF16))
        ps_b = psum_rot(nc, s2, nm + "_psb", 2, [128, TC])
        ob1 = p.v(lambda e: e.memset(ones_rb[:, :], 1.0), inc="v")
        ob2 = p.v(lambda e: e.engine_nop(), wait=[ob1], inc="v")
        p.dma("scalar", posr[:, :], W["pos_row"][0:1, :])
        posf = p.fence("scalar")
        btoks = emit_bcast(p, s2, nm + "_bp", posr, T, posb, 0, [posf, ob2],
                           ps_b, ones_rb, ob2)
        oneap, onetok = p.constp(1.0)
        prev = [lr2] + btoks
        for dt in range(NDT):
            sl = slice(dt * 128, (dt + 1) * 128)
            p.dma("sync", y0t[:, :], y0_dr[sl, :], wait=prev)
            p.dma("sync", Vt[:, :], V_dr[sl, :], wait=prev)
            p.dma("sync", Kt[:, :], Kf_dr[sl, :], wait=prev)
            p.dma("sync", Qt[:, :], Q_dr[sl, :], wait=prev)
            p.dma("sync", gt[:, :], gate_dr[sl, :], wait=prev)
            df = p.fence("sync")
            z1 = p.v(lambda e: e.memset(cs[:, 0:1], 0.0), wait=prev, inc="v")
            z2 = p.v(lambda e: e.engine_nop(), wait=[z1], inc="v")
            e1 = p.v(lambda e: e.tensor_sub(t1[:, :], y0t[:, :], Vt[:, :]),
                     wait=[df, z2], inc="v")
            e2 = p.v(lambda e: e.tensor_mul(t1[:, :], Kt[:, :], t1[:, :]),
                     wait=[e1], inc="v")
            e2b = p.v(lambda e: e.engine_nop(), wait=[e2], inc="v")
            sc = p.v(lambda e, ob=oneap.broadcast_to([128, T]):
                     e.tensor_tensor_scan(cs[:, 1:T + 1], ob, t1[:, :], 0.0,
                                          OP.mult, OP.add),
                     wait=[e2b, onetok], inc="v")
            scb = p.v(lambda e: e.engine_nop(), wait=[sc], inc="v")
            e3 = p.v(lambda e, lr=lr_t[:, dt:dt + 1]: e.scalar_tensor_tensor(
                t2[:, :], Qt[:, :], lr, cs[:, 0:T], OP.mult, OP.mult),
                wait=[scb], inc="v")
            e4 = p.v(lambda e: e.tensor_mul(t2[:, :], t2[:, :], posb[:, :]),
                     wait=[e3], inc="v")
            e5 = p.v(lambda e: e.tensor_sub(t1[:, :], y0t[:, :], t2[:, :]),
                     wait=[e4], inc="v")
            e6 = p.v(lambda e: e.tensor_mul(yst[:, :], t1[:, :], gt[:, :]),
                     wait=[e5], inc="v")
            p.dma("gpsimd", ob_dr[sl, :], yst[:, :], wait=[e6])
            yf = p.fence("gpsimd")
            prev = [yf, e6]
    p.set_ready(prev)

    with ExitStack() as s3:
        toks = emit_mm(p, s3, nm + "_o", [(xnew_dram, F32)],
                       [(W["WoT"], F32, ob_dr, D)], D, T,
                       scale_pp=lambda dt: (par[:, 3 * NDT + dt:3 * NDT + dt + 1],
                                            parf),
                       res_dram=(x_dram, xp_f))
    p.set_ready(toks)
    return toks


def emit_cast_dram(p, st, nm, src_dr, dst_dr, rows, cols):
    nc = p.nc
    stage = sbuf_rot(nc, st, nm + "_cf", 2, [128, cols], F32)
    stgb = sbuf_rot(nc, st, nm + "_cb", 2, [128, cols], BF16)
    toks = []
    for rt in range(ceil_div(rows, 128)):
        rp = min(128, rows - rt * 128)
        fi, ft, fdeps = stage.next()
        p.dma("sync", ft[0:rp, :], src_dr[rt * 128:rt * 128 + rp, :], wait=fdeps)
        ff = p.fence("sync")
        bi, bt, bdeps = stgb.next()
        ct = p.v(lambda e, o=bt, i=ft, rp=rp: e.tensor_copy(o[0:rp, :], i[0:rp, :]),
                 wait=bdeps + [ff], inc="v")
        stage.add_dep(fi, ct)
        p.dma("gpsimd", dst_dr[rt * 128:rt * 128 + rp, :], bt[0:rp, :], wait=[ct])
        gf = p.fence("gpsimd")
        stgb.add_dep(bi, gf)
        toks.append(gf)
    return toks



def emit_wattn(p, st, nm, x_dram, xnew_dram, W, scr_dram):
    nc = p.nc
    H, hd, ws = 16, 64, 256
    nw = T // ws
    SCALE = hd ** (-0.5)
    xp_bf = scr_dram(nm + "_xp", [D, T], BF16)
    Q_dr = scr_dram(nm + "_Q", [D, T], BF16)
    K_dr = scr_dram(nm + "_K", [hd, T], BF16)
    V_dr = scr_dram(nm + "_V", [T, hd], BF16)
    O_dr = scr_dram(nm + "_O", [D, T], BF16)
    kvw_bf = scr_dram(nm + "_kvb", [D, 128], BF16)

    with ExitStack() as s0:
        t0 = emit_ln(p, s0, nm + "_ln", x_dram, (), [(xp_bf, BF16)])
    p.set_ready(t0)
    with ExitStack() as s0:
        ct = emit_cast_dram(p, s0, nm + "_cv", W["kv_wT"], kvw_bf, D, 128)
    p.set_ready(ct)
    with ExitStack() as s0:
        cache = {}
        tq = emit_mm(p, s0, nm + "_q", [(Q_dr, BF16)], [(W["q_wT"], F32, xp_bf, D)],
                     D, T, rhs_cache=cache)
        tk = emit_mm(p, s0, nm + "_k", [(K_dr, BF16)],
                     [(W["kv_wT"][:, 0:hd], F32, xp_bf, D)], hd, T, rhs_cache=cache)
        tv = emit_mm(p, s0, nm + "_v", [(V_dr, BF16)],
                     [(xp_bf, BF16, kvw_bf[:, hd:128], D)], T, hd)
    p.set_ready(tq + tk + tv)

    idn = nc.inline_tensor(np.eye(128, dtype=np.float32), name=nm + "_id").ap()
    with ExitStack() as s1:
        scap, sctok = p.constp(SCALE)
        idb = s1.enter_context(nc.sbuf_tensor(nm + "_idb", [128, 128], BF16))
        idf = s1.enter_context(nc.sbuf_tensor(nm + "_idf", [128, 128], F32))
        p.dma("sync", idf[:, :], idn)
        idfc = p.fence("sync")
        idt = p.v(lambda e: e.tensor_copy(idb[:, :], idf[:, :]), wait=[idfc], inc="v")
        qwin = sbuf_rot(nc, s1, nm + "_qw", 2 * 16, [64, ws], BF16)
        kwin = sbuf_rot(nc, s1, nm + "_kw", 2, [64, ws], BF16)
        vwin = sbuf_rot(nc, s1, nm + "_vw", 4, [128, hd], BF16)
        ps_s = psum_rot(nc, s1, nm + "_pss", 2, [128, ws])
        ps_t = Rot([s1.enter_context(nc.psum_tensor(f"{nm}_pst{i}", [128, ws], BF16))
                    for i in range(2)])
        ps_o = psum_rot(nc, s1, nm + "_pso", 2, [64, 128])
        rmx = sbuf_rot(nc, s1, nm + "_rm", 4, [128, 2], F32)
        pt_sb = sbuf_rot(nc, s1, nm + "_pt", 2, [128, ws], BF16)
        p_sb = sbuf_rot(nc, s1, nm + "_p", 2, [128, ws], BF16)
        ost = sbuf_rot(nc, s1, nm + "_ost", 32, [64, 128], BF16)
        ost_used = []
        for w in range(nw):
            cols = slice(w * ws, (w + 1) * ws)
            qt_tiles = []
            for h2 in range(16):
                qi, qt_, qdeps = qwin.next()
                p.dma("sync", qt_[:, :], Q_dr[h2 * 64:(h2 + 1) * 64, cols],
                      wait=qdeps)
                qt_tiles.append((qi, qt_))
            ki, kt_, kdeps = kwin.next()
            p.dma("sync", kt_[:, :], K_dr[:, cols], wait=kdeps)
            vts = []
            for j in range(2):
                vi, vt_, vdeps = vwin.next()
                p.dma("sync", vt_[:, :], V_dr[w * ws + j * 128:w * ws + (j + 1) * 128, :],
                      wait=vdeps)
                vts.append((vi, vt_))
            wf = p.fence("sync")
            for h in range(H):
                qtile = qt_tiles[h][1]
                for qt2 in range(2):
                    pi, ps, pdeps = ps_s.next()
                    smm = p.t(lambda e, ps=ps, q=qtile[:, qt2 * 128:(qt2 + 1) * 128],
                              k=kt_[:, :]:
                              e.matmul(ps[:, :], q, k, start=True, stop=True),
                              wait=pdeps + [wf, idt], inc="t")
                    ri, rm, rdeps = rmx.next()
                    r1 = p.v(lambda e, rm=rm, ps=ps:
                             e.reduce_max(rm[:, 0:1], ps[:, :], axis=AX.X),
                             wait=rdeps + [smm], inc="v")
                    r2 = p.v(lambda e, rm=rm: e.tensor_scalar_mul(
                        rm[:, 1:2], rm[:, 0:1], -SCALE), wait=[r1], inc="v")
                    bi2, pt2, pdeps2 = p_sb.next()
                    ex = p.s(lambda e, o=pt2, ps=ps, rm=rm, sc=scap:
                             e.activation(o[:, :], ps[:, :], AF.Exp,
                                          bias=rm[:, 1:2], scale=sc,
                                          accum_out=rm[:, 0:1]),
                             wait=pdeps2 + [r2, sctok], inc="a")
                    ps_s.add_dep(pi, ex)
                    rc = p.v(lambda e, rm=rm: e.reciprocal(rm[:, 1:2], rm[:, 0:1]),
                             wait=[ex], inc="v")
                    rcb = p.v(lambda e: e.engine_nop(), wait=[rc], inc="v")
                    pn = p.v(lambda e, o=pt2, rm=rm: e.tensor_scalar_mul(
                        o[:, :], o[:, :], rm[:, 1:2]), wait=[rcb], inc="v")
                    pnb = p.v(lambda e: e.engine_nop(), wait=[pn], inc="v")
                    rmx.add_dep(ri, pn)
                    ti2, pst, tdeps = ps_t.next()
                    tt1 = p.t(lambda e, o=pst[:, 0:128], i=pt2[:, 0:128], idb=idb:
                              e.transpose(o, i, idb[:, :]), wait=tdeps + [pnb], inc="t")
                    tt2 = p.t(lambda e, o=pst[:, 128:256], i=pt2[:, 128:256], idb=idb:
                              e.transpose(o, i, idb[:, :]), wait=[tt1], inc="t")
                    p_sb.add_dep(bi2, tt2)
                    pti, ptt, ptdeps = pt_sb.next()
                    pe1 = p.v(lambda e, o=ptt, i=pst: e.tensor_copy(o[:, :], i[:, :]),
                              wait=ptdeps + [tt2], inc="v")
                    ps_t.add_dep(ti2, pe1)
                    oi, pso, odeps = ps_o.next()
                    omm = None
                    for j in range(2):
                        omm = p.t(lambda e, ps=pso, v=vts[j][1],
                                  r=ptt[:, j * 128:(j + 1) * 128],
                                  f=(j == 0), l=(j == 1):
                                  e.matmul(ps[:, :], v[:, :], r, start=f, stop=l),
                                  wait=(odeps if j == 0 else []) + [pe1],
                                  inc="t" if j == 1 else None)
                    pt_sb.add_dep(pti, omm)
                    osi, ot_, osdeps = ost.next()
                    oe = p.s(lambda e, o=ot_, ps=pso: e.copy(o[:, :], ps[:, :]),
                             wait=osdeps + [omm], inc="a")
                    ps_o.add_dep(oi, oe)
                    p.dma("gpsimd",
                          O_dr[h * 64:(h + 1) * 64,
                               w * ws + qt2 * 128:w * ws + (qt2 + 1) * 128],
                          ot_[:, :], wait=[oe])
                    ost_used.append(osi)
            gf = p.fence("gpsimd")
            for osi in ost_used:
                ost.add_dep(osi, gf)
            ost_used = []
            for (qi, qt_) in qt_tiles:
                qwin.add_dep(qi, ("t", p.cnt["t"]))
            kwin.add_dep(ki, ("t", p.cnt["t"]))
            for (vi, vt_) in vts:
                vwin.add_dep(vi, ("t", p.cnt["t"]))
    p.set_ready([gf])

    with ExitStack() as s3:
        toks = emit_mm(p, s3, nm + "_o", [(xnew_dram, F32)],
                       [(W["proj_wT"], F32, O_dr, D), (W["proj_b"], F32, None, 1)],
                       D, T, res_dram=(x_dram,))
    p.set_ready(toks)
    return toks



def emit_mm_bigK(p, st, nm, out_specs, pairs, M, N, act=None, res_dram=()):
    nc = p.nc
    n_mt = ceil_div(M, 128)
    n_nc = ceil_div(N, TC)
    wres = []
    wstage = sbuf_rot(nc, st, nm + "_wf", 4, [128, 128], F32)
    wready = []
    for j, (lT, rhs, K) in enumerate(pairs):
        nk = ceil_div(K, 128)
        wt = st.enter_context(nc.sbuf_tensor(f"{nm}_w{j}", [128, nk * M], BF16))
        wres.append(wt)
        cts = []
        for kk in range(nk):
            for mt in range(n_mt):
                mp = min(128, M - mt * 128)
                fi, ftile, fdeps = wstage.next()
                p.dma("sync", ftile[:, 0:mp],
                      lT[kk * 128:kk * 128 + min(128, K - kk * 128),
                         mt * 128:mt * 128 + mp], wait=fdeps)
                wfc = p.fence("sync")
                ct = p.v(lambda e, o=wt[:, (kk * n_mt + mt) * 128:
                                        (kk * n_mt + mt) * 128 + mp],
                         i=ftile[:, 0:mp]: e.tensor_copy(o, i),
                         wait=[wfc], inc="v")
                wstage.add_dep(fi, ct)
                cts.append(ct)
        wready.append(cts[-1])
    rbufs = []
    for j, (lT, rhs, K) in enumerate(pairs):
        nk = ceil_div(K, 128)
        rbufs.append(sbuf_rot(nc, st, f"{nm}_r{j}", nk, [128, TC], BF16))
    pss = psum_rot(nc, st, nm + "_ps", 2, [128, TC])
    scr = sbuf_rot(nc, st, nm + "_sc", 4, [128, TC], F32)
    res_bufs = [sbuf_rot(nc, st, f"{nm}_rb{k}", n_mt + 1, [128, TC], F32)
                for k in range(len(res_dram))]
    out_bufs = [sbuf_rot(nc, st, f"{nm}_ob{oi}", n_mt + 1, [128, TC], odt)
                for oi, (_, odt) in enumerate(out_specs)]
    out_toks = []
    owar = None
    for ncc in range(n_nc):
        np_ = min(TC, N - ncc * TC)
        rslots = []
        for j, (lT, rhs, K) in enumerate(pairs):
            nk = ceil_div(K, 128)
            slots = []
            for kk in range(nk):
                kp = min(128, K - kk * 128)
                ri, rt_, rdeps = rbufs[j].next()
                p.dma("scalar", rt_[0:kp, 0:np_],
                      rhs[kk * 128:kk * 128 + kp, ncc * TC:ncc * TC + np_],
                      wait=rdeps)
                slots.append((ri, rt_, kp))
            rslots.append(slots)
        rf = p.fence("scalar")
        res_slots = []
        for k, rd in enumerate(res_dram):
            for mt in range(n_mt):
                mp = min(128, M - mt * 128)
                ri2, rtile, rdeps2 = res_bufs[k].next()
                p.dma("sync", rtile[0:mp, 0:np_],
                      rd[mt * 128:mt * 128 + mp, ncc * TC:ncc * TC + np_],
                      wait=rdeps2)
                res_slots.append((k, mt, ri2, rtile))
        resf = p.fence("sync") if res_dram else None
        for mt in range(n_mt):
            mp = min(128, M - mt * 128)
            pi, ps, pdeps = pss.next()
            tot = sum(len(s) for s in rslots)
            cnt = 0
            mm_tok = None
            for j, (lT, rhs, K) in enumerate(pairs):
                for kk, (ri, rt_, kp) in enumerate(rslots[j]):
                    cnt += 1
                    w = ([rf, wready[j]] + pdeps) if cnt == 1 else []
                    mm_tok = p.t(lambda e, ps=ps,
                                 b=wres[j][:, (kk * n_mt + mt) * 128:
                                           (kk * n_mt + mt) * 128 + mp],
                                 r=rt_[0:kp, 0:np_], mp=mp, np_=np_,
                                 f=(cnt == 1), l=(cnt == tot):
                                 e.matmul(ps[0:mp, 0:np_], b, r, start=f, stop=l),
                                 wait=w, inc="t" if cnt == tot else None)
            for j in range(len(pairs)):
                for (ri, rt_, kp) in rslots[j]:
                    rbufs[j].add_dep(ri, mm_tok)
            si, stile, sdeps = scr.next()
            ev = p.s(lambda e, s=stile[0:mp, 0:np_], ps=ps[0:mp, 0:np_],
                     a=(act if act else AF.Copy): e.activation(s, ps, a),
                     wait=sdeps + [mm_tok], inc="a")
            pss.add_dep(pi, ev)
            cur, cur_tok = stile, ev
            ci = si
            for k, rd in enumerate(res_dram):
                for (k2, mt2, ri2, rtile) in res_slots:
                    if k2 == k and mt2 == mt:
                        rslot, rtl = ri2, rtile
                c2, ct2, cdeps2 = scr.next()
                ntok = p.v(lambda e, o=ct2[0:mp, 0:np_], a=cur[0:mp, 0:np_],
                           b=rtl[0:mp, 0:np_]: e.tensor_add(o, a, b),
                           wait=cdeps2 + [cur_tok, resf], inc="v")
                scr.add_dep(ci, ntok)
                res_bufs[k].add_dep(rslot, ntok)
                cur, cur_tok, ci = ct2, ntok, c2
            for oi, (odr, odt) in enumerate(out_specs):
                bi, btile, bdeps = out_bufs[oi].next()
                ktok = p.v(lambda e, o=btile[0:mp, 0:np_], i=cur[0:mp, 0:np_]:
                           e.tensor_copy(o, i),
                           wait=bdeps + [cur_tok, owar], inc="v")
                scr.add_dep(ci, ktok)
                wtok = p.dma("gpsimd",
                             odr[mt * 128:mt * 128 + mp, ncc * TC:ncc * TC + np_],
                             btile[0:mp, 0:np_], wait=[ktok])
                out_toks.append(wtok)
        owar = p.fence("gpsimd")
    out_toks.append(owar)
    return out_toks



def emit_ffn_h(p, st, nm, h_dr, W1_dr, rhs1_dr, act, M,
               second=None, gateb=None, rhs_cache=None):
    nc = p.nc
    n_mt = ceil_div(M, 128)
    rhs_tiles = {}
    loads = False
    for rhs in ([rhs1_dr] + ([second[1]] if second else [])):
        key = id(rhs.tensor)
        if key in rhs_tiles:
            continue
        if rhs_cache is not None and key in rhs_cache:
            rhs_tiles[key] = rhs_cache[key]
            continue
        tiles = []
        for kk in range(NDT):
            tile = st.enter_context(
                nc.sbuf_tensor(f"{nm}_rhs{len(rhs_tiles)}_{kk}", [128, T], BF16))
            p.dma("sync", tile[:, :], rhs[kk * 128:(kk + 1) * 128, :])
            tiles.append(tile)
            loads = True
        rhs_tiles[key] = tiles
        if rhs_cache is not None:
            rhs_cache[key] = tiles
    rtok = p.fence("sync") if loads else None
    wstage = sbuf_rot(nc, st, nm + "_wf", NDT * (2 if second else 1), [128, 128], F32)
    wchk = sbuf_rot(nc, st, nm + "_wb", NDT * (2 if second else 1), [128, 128], BF16)
    ps1 = psum_rot(nc, st, nm + "_p1", 2, [128, TC])
    ps2 = psum_rot(nc, st, nm + "_p2", 2, [128, TC]) if second else None
    stg = sbuf_rot(nc, st, nm + "_sg", 3, [128, TC], F32)
    hst = sbuf_rot(nc, st, nm + "_hs", NTC, [128, TC], BF16)
    out_toks = []
    owar = None
    for mt in range(n_mt):
        wd = []
        for wj, wdr_ in enumerate([W1_dr] + ([second[0]] if second else [])):
            for kk in range(NDT):
                fi, ftile, fdeps = wstage.next()
                p.dma("scalar", ftile[:, :],
                      wdr_[kk * 128:(kk + 1) * 128, mt * 128:(mt + 1) * 128],
                      wait=fdeps)
                wd.append((wj, kk, fi, ftile))
        wfc = p.fence("scalar")
        wchunks = {}
        for (wj, kk, fi, ftile) in wd:
            bi, btile, bdeps = wchk.next()
            ct = p.v(lambda e, o=btile, i=ftile: e.tensor_copy(o[:, :], i[:, :]),
                     wait=bdeps + [wfc], inc="v")
            wstage.add_dep(fi, ct)
            wchunks[(wj, kk)] = (btile, bi, ct)
        r1tiles = rhs_tiles[id(rhs1_dr.tensor)]
        r2tiles = rhs_tiles[id(second[1].tensor)] if second else None
        for tc in range(NTC):
            pi1, psA, pd1 = ps1.next()
            g_tok = None
            for kk in range(NDT):
                btile, bi, ct = wchunks[(0, kk)]
                g_tok = p.t(lambda e, ps=psA, b=btile,
                            r=r1tiles[kk][:, tc * TC:(tc + 1) * TC],
                            f=(kk == 0), l=(kk == NDT - 1):
                            e.matmul(ps[:, :], b[:, :], r, start=f, stop=l),
                            wait=(pd1 + [rtok, ct]) if kk == 0 else [ct],
                            inc="t" if kk == NDT - 1 else None)
            u_tok = None
            if second:
                pi2, psB, pd2 = ps2.next()
                for kk in range(NDT):
                    btile, bi, ct = wchunks[(1, kk)]
                    u_tok = p.t(lambda e, ps=psB, b=btile,
                                r=r2tiles[kk][:, tc * TC:(tc + 1) * TC],
                                f=(kk == 0), l=(kk == NDT - 1):
                                e.matmul(ps[:, :], b[:, :], r, start=f, stop=l),
                                wait=(pd2 + [ct]) if kk == 0 else [ct],
                                inc="t" if kk == NDT - 1 else None)
            si, stile, sdeps = stg.next()
            ev = p.s(lambda e, s=stile, ps=psA, a=act: e.activation(s[:, :],
                                                                   ps[:, :], a),
                     wait=sdeps + [g_tok], inc="a")
            ps1.add_dep(pi1, ev)
            hi, ht, hdeps = hst.next()
            if second:
                cm = p.v(lambda e, o=ht, a=stile, b=psB:
                         e.tensor_mul(o[:, :], a[:, :], b[:, :]),
                         wait=hdeps + [ev, u_tok, owar], inc="v")
                ps2.add_dep(pi2, cm)
            else:
                gb, gbtok = gateb[tc]
                cm = p.v(lambda e, o=ht, a=stile, b=gb:
                         e.tensor_mul(o[:, :], a[:, :], b),
                         wait=hdeps + [ev, gbtok, owar], inc="v")
            stg.add_dep(si, cm)
            p.dma("gpsimd", h_dr[mt * 128:(mt + 1) * 128, tc * TC:(tc + 1) * TC],
                  ht[:, :], wait=[cm])
            hst.add_dep(hi, None)
        owar = p.fence("gpsimd")
        for i2 in range(len(hst.tiles)):
            hst.add_dep(i2, owar)
        for key2 in wchunks:
            wchk.add_dep(wchunks[key2][1], ("t", p.cnt["t"]))
    out_toks.append(owar)
    return out_toks



def emit_moe(p, st, nm, x_dram, xnew_dram, W, scr_dram, dbg=None):
    nc = p.nc
    E, H = 8, 4096
    xp_bf = scr_dram(nm + "_xp", [D, T], BF16)
    wg_bf = scr_dram(nm + "_wgb", [D, 8], BF16)
    bg_bf = scr_dram(nm + "_bgb", [1, 8], BF16)
    logits_dr = scr_dram(nm + "_lg", [T, 8], F32)
    grow_dr = scr_dram(nm + "_gr", [8, T], BF16)
    h_dr = scr_dram(nm + "_h", [H, T], BF16)
    acc = [scr_dram(nm + "_a0", [D, T], F32), scr_dram(nm + "_a1", [D, T], F32)]

    with ExitStack() as s0:
        t0 = emit_ln(p, s0, nm + "_ln", x_dram, (), [(xp_bf, BF16)])
    p.set_ready(t0)
    with ExitStack() as s0:
        c1 = emit_cast_dram(p, s0, nm + "_c1", W["wgT"], wg_bf, D, 8)
        c2 = emit_cast_dram(p, s0, nm + "_c2", W["bg"], bg_bf, 1, 8)
    p.set_ready(c1 + c2)
    with ExitStack() as s0:
        tl = emit_mm(p, s0, nm + "_lg", [(logits_dr, F32)],
                     [(xp_bf, BF16, wg_bf, D), (W["ones_row"], F32, bg_bf, 1)],
                     T, 8)
    p.set_ready(tl)

    idn = nc.inline_tensor(np.eye(128, dtype=np.float32), name=nm + "_id").ap()
    with ExitStack() as s1:
        lg = s1.enter_context(nc.sbuf_tensor(nm + "_lgt", [128, 32 * 8], F32))
        gt = s1.enter_context(nc.sbuf_tensor(nm + "_gt", [128, 32 * 8], BF16))
        mx = sbuf_rot(nc, s1, nm + "_mx", 4, [128, 8], F32)
        m8p = sbuf_rot(nc, s1, nm + "_m8", 4, [128, 8], U8)
        selp = sbuf_rot(nc, s1, nm + "_sel", 4, [128, 8], F32)
        ppp = sbuf_rot(nc, s1, nm + "_pp", 4, [128, 8], F32)
        rsp = sbuf_rot(nc, s1, nm + "_rs", 4, [128, 2], F32)
        negt = s1.enter_context(nc.sbuf_tensor(nm + "_ng", [128, 8], F32))
        idb = s1.enter_context(nc.sbuf_tensor(nm + "_idb", [128, 128], BF16))
        idf = s1.enter_context(nc.sbuf_tensor(nm + "_idf", [128, 128], F32))
        pst = Rot([s1.enter_context(nc.psum_tensor(f"{nm}_pt{i}", [8, 128], BF16))
                   for i in range(2)])
        gst = sbuf_rot(nc, s1, nm + "_gs", 4, [8, 128], BF16)
        p.dma("sync", idf[:, :], idn)
        for j in range(32):
            p.dma("sync", lg[:, j * 8:(j + 1) * 8],
                  logits_dr[j * 128:(j + 1) * 128, :])
        lf = p.fence("sync")
        idt = p.v(lambda e: e.tensor_copy(idb[:, :], idf[:, :]), wait=[lf], inc="v")
        ng1 = p.v(lambda e: e.memset(negt[:, :], -30000.0), wait=[idt], inc="v")
        ng2 = p.v(lambda e: e.engine_nop(), wait=[ng1], inc="v")
        gtoks = []
        for j in range(32):
            lgj = lg[:, j * 8:(j + 1) * 8]
            mi, mxt, mdeps = mx.next()
            t1 = p.v(lambda e, o=mxt, i=lgj: e.max(out=o[:, :], in_=i),
                     wait=mdeps + [lf, ng2], inc="v")
            bi8, m8t, m8deps = m8p.next()
            t2 = p.v(lambda e, o=m8t, i=lgj, v1=mxt[:, 1:2].broadcast_to([128, 8]):
                     e.tensor_tensor(o[:, :], i, v1, OP.is_ge),
                     wait=m8deps + [t1], inc="v")
            si, selt, sdeps = selp.next()
            t3 = p.v(lambda e, o=selt, m=m8t, i=lgj:
                     e.select(o[:, :], m[:, :], i, negt[:, :]),
                     wait=sdeps + [t2], inc="v")
            ri, rst, rdeps = rsp.next()
            t4 = p.v(lambda e, o=rst, m=mxt: e.tensor_scalar_mul(
                o[:, 0:1], m[:, 0:1], -1.0), wait=rdeps + [t3], inc="v")
            pi2, ppt, ppdeps = ppp.next()
            t5 = p.s(lambda e, o=ppt, i=selt, b=rst, r=rst:
                     e.activation(o[:, :], i[:, :], AF.Exp, bias=b[:, 0:1],
                                  accum_out=r[:, 1:2]),
                     wait=[t4] + ppdeps, inc="a")
            selp.add_dep(si, t5)
            m8p.add_dep(bi8, t5)
            mx.add_dep(mi, t5)
            t6 = p.v(lambda e, r=rst: e.reciprocal(r[:, 0:1], r[:, 1:2]),
                     wait=[t5], inc="v")
            t6b = p.v(lambda e: e.engine_nop(), wait=[t6], inc="v")
            t7 = p.v(lambda e, o=gt[:, j * 8:(j + 1) * 8], i=ppt, r=rst:
                     e.tensor_scalar_mul(o, i[:, :], r[:, 0:1]),
                     wait=[t6b], inc="v")
            ppp.add_dep(pi2, t7)
            rsp.add_dep(ri, t7)
            pi, pt_, pdeps = pst.next()
            tt = p.t(lambda e, o=pt_, i=gt[:, j * 8:(j + 1) * 8]:
                     e.transpose(o[:, :], i, idb[:, :]),
                     wait=pdeps + [t7], inc="t")
            gi, gs_, gdeps = gst.next()
            ge = p.v(lambda e, o=gs_, i=pt_: e.tensor_copy(o[:, :], i[:, :]),
                     wait=gdeps + [tt], inc="v")
            pst.add_dep(pi, ge)
            p.dma("gpsimd", grow_dr[:, j * 128:(j + 1) * 128], gs_[:, :],
                  wait=[ge])
            gtoks.append(gi)
        gf = p.fence("gpsimd")
    p.set_ready([gf])

    xp_tiles = []
    for kk in range(NDT):
        tile = st.enter_context(nc.sbuf_tensor(f"{nm}_xpt{kk}", [128, T], BF16))
        p.dma("sync", tile[:, :], xp_bf[kk * 128:(kk + 1) * 128, :])
        xp_tiles.append(tile)
    xpf = p.fence("sync")
    cache_xp = {id(xp_bf.tensor): xp_tiles}

    prev_toks = [xpf]
    for e in range(E):
        res_in = x_dram if e == 0 else acc[(e + 1) % 2]
        out_dr = xnew_dram if e == E - 1 else acc[e % 2]
        with ExitStack() as se:
            ones_rb = se.enter_context(nc.sbuf_tensor(f"{nm}_ob{e}", [1, 128], BF16))
            grow_sb = se.enter_context(nc.sbuf_tensor(f"{nm}_gr{e}", [1, T], BF16))
            o1 = p.v(lambda e2: e2.memset(ones_rb[:, :], 1.0),
                     wait=prev_toks, inc="v")
            o2 = p.v(lambda e2: e2.engine_nop(), wait=[o1], inc="v")
            p.dma("scalar", grow_sb[:, :], grow_dr[e:e + 1, :], wait=prev_toks)
            grf = p.fence("scalar")
            psb = psum_rot(nc, se, f"{nm}_pb{e}", 2, [128, TC])
            gateb = []
            for tc in range(NTC):
                pi, ps, pdeps = psb.next()
                bm = p.t(lambda e2, ps=ps, r=grow_sb[0:1, tc * TC:(tc + 1) * TC]:
                         e2.matmul(ps[:, :], ones_rb[:, :], r, start=True, stop=True),
                         wait=pdeps + [o2, grf], inc="t")
                gtile = se.enter_context(
                    nc.sbuf_tensor(f"{nm}_gb{e}_{tc}", [128, TC], BF16))
                ev = p.v(lambda e2, o=gtile, ps=ps: e2.tensor_copy(o[:, :], ps[:, :]),
                         wait=[bm], inc="v")
                psb.add_dep(pi, ev)
                gateb.append((gtile[:, :], ev))
            xg_tiles = []
            xgtoks = []
            for kk in range(NDT):
                xgt = se.enter_context(
                    nc.sbuf_tensor(f"{nm}_xg{e}_{kk}", [128, T], BF16))
                for tc in range(NTC):
                    tkx = p.v(lambda e2, o=xgt[:, tc * TC:(tc + 1) * TC],
                              a=xp_tiles[kk][:, tc * TC:(tc + 1) * TC],
                              b=gateb[tc][0]:
                              e2.tensor_mul(o, a, b),
                              wait=[xpf, gateb[tc][1]], inc="v")
                    xgtoks.append(tkx)
                xg_tiles.append(xgt)
            xg_key_dr = scr_dram(f"{nm}_xgd{e}", [8, 8], BF16)
            cache_e = dict(cache_xp)
            cache_e[id(xg_key_dr.tensor)] = xg_tiles
            p.set_ready(xgtoks)
            if dbg is not None and e == 0:
                for kk in range(NDT):
                    p.dma("gpsimd", dbg["xg"][kk * 128:(kk + 1) * 128, :],
                          xg_tiles[kk][:, :], wait=xgtoks)
                for tc in range(NTC):
                    p.dma("gpsimd", dbg["gb"][:, tc * TC:(tc + 1) * TC],
                          gateb[tc][0], wait=[gateb[tc][1]])
                p.set_ready([p.fence("gpsimd")])
            th = emit_ffn_h(p, se, f"{nm}_h{e}", h_dr,
                            W["WgateT"][e * D:(e + 1) * D, :], xp_bf, AF.Silu, H,
                            second=(W["WupT"][e * D:(e + 1) * D, :], xg_key_dr),
                            rhs_cache=cache_e)
            p.set_ready(th)
            if dbg is not None and e == 0:
                with ExitStack() as sd:
                    hc = emit_cast_dram(p, sd, nm + "_hd", h_dr, None, 0, 0) \
                        if False else None
                p.dma("gpsimd", dbg["h"], h_dr[:, :])
                p.set_ready([p.fence("gpsimd")])
        with ExitStack() as se2:
            td = emit_mm_bigK(p, se2, f"{nm}_d{e}", [(out_dr, F32)],
                              [(W["WdownT"][e * H:(e + 1) * H, :], h_dr, H)],
                              D, T, res_dram=(res_in,))
        p.set_ready(td)
        prev_toks = td
    return prev_toks



def emit_ffn0(p, st, nm, x_dram, xnew_dram, W, scr_dram, n_cores=8):
    nc = p.nc
    E, F = 8, 2048
    TT = n_cores * T
    cap = TT // E
    xp_bf = scr_dram(nm + "_xp", [D, T], BF16)
    wg_bf = scr_dram(nm + "_wgb", [D, 8], BF16)
    logits_dr = scr_dram(nm + "_lg", [T, 8], F32)
    probs_dr = nc.dram_tensor(nm + "_pl", [T, 8], F32, kind="Internal").ap()
    gall_dr = nc.dram_tensor(nm + "_ga", [TT, 8], F32, kind="Internal").ap()
    grow_dr = scr_dram(nm + "_gr", [8, T], BF16)
    h_dr = scr_dram(nm + "_h", [F, T], BF16)
    acc = [scr_dram(nm + "_a0", [D, T], F32), scr_dram(nm + "_a1", [D, T], F32)]

    with ExitStack() as s0:
        t0 = emit_ln(p, s0, nm + "_ln", x_dram, (), [(xp_bf, BF16)])
    p.set_ready(t0)
    with ExitStack() as s0:
        c1 = emit_cast_dram(p, s0, nm + "_c1", W["Wg"], wg_bf, D, 8)
    p.set_ready(c1)
    with ExitStack() as s0:
        tl = emit_mm(p, s0, nm + "_lgm", [(logits_dr, F32)],
                     [(xp_bf, BF16, wg_bf, D)], T, 8)
    p.set_ready(tl)

    idn = nc.inline_tensor(np.eye(128, dtype=np.float32), name=nm + "_id").ap()
    with ExitStack() as s1:
        lg = s1.enter_context(nc.sbuf_tensor(nm + "_lgt", [128, 32 * 8], F32))
        probs = s1.enter_context(nc.sbuf_tensor(nm + "_pr", [128, 32 * 8], F32))
        gt = s1.enter_context(nc.sbuf_tensor(nm + "_gt", [128, 32 * 8], BF16))
        mx = sbuf_rot(nc, s1, nm + "_mx", 4, [128, 8], F32)
        rsp = sbuf_rot(nc, s1, nm + "_rs", 4, [128, 2], F32)
        idb = s1.enter_context(nc.sbuf_tensor(nm + "_idb", [128, 128], BF16))
        idf = s1.enter_context(nc.sbuf_tensor(nm + "_idf", [128, 128], F32))
        p.dma("sync", idf[:, :], idn)
        for j in range(32):
            p.dma("sync", lg[:, j * 8:(j + 1) * 8],
                  logits_dr[j * 128:(j + 1) * 128, :])
        lf = p.fence("sync")
        idt = p.v(lambda e: e.tensor_copy(idb[:, :], idf[:, :]), wait=[lf], inc="v")
        ptoks = []
        for j in range(32):
            lgj = lg[:, j * 8:(j + 1) * 8]
            prj = probs[:, j * 8:(j + 1) * 8]
            mi, mxt, mdeps = mx.next()
            t1 = p.v(lambda e, o=mxt, i=lgj: e.reduce_max(o[:, 0:1], i, axis=AX.X),
                     wait=mdeps + [lf, idt], inc="v")
            t2 = p.v(lambda e, m=mxt: e.tensor_scalar_mul(m[:, 1:2], m[:, 0:1], -1.0),
                     wait=[t1], inc="v")
            ri, rst, rdeps = rsp.next()
            t3 = p.s(lambda e, o=prj, i=lgj, m=mxt, r=rst:
                     e.activation(o, i, AF.Exp, bias=m[:, 1:2],
                                  accum_out=r[:, 0:1]),
                     wait=rdeps + [t2], inc="a")
            mx.add_dep(mi, t3)
            t4 = p.v(lambda e, r=rst: e.reciprocal(r[:, 1:2], r[:, 0:1]),
                     wait=[t3], inc="v")
            t4b = p.v(lambda e: e.engine_nop(), wait=[t4], inc="v")
            t5 = p.v(lambda e, o=prj, r=rst: e.tensor_scalar_mul(o, o, r[:, 1:2]),
                     wait=[t4b], inc="v")
            rsp.add_dep(ri, t5)
            wt_ = p.dma("gpsimd", probs_dr[j * 128:(j + 1) * 128, :], prj,
                        wait=[t5])
            ptoks.append(wt_)
        pf = p.fence("gpsimd")
        cctok = p.op("gpsimd", lambda e: e.collective_compute(
            "AllGather", mybir.AluOpType.bypass,
            replica_groups=[list(range(n_cores))],
            ins=[probs_dr.opt()], outs=[gall_dr.opt()]), wait=[pf], inc="c")
        npp = TT * 8 // 128
        pa = s1.enter_context(nc.sbuf_tensor(nm + "_pa", [128, npp], F32))
        p.wait("gpsimd", [cctok])
        p.dma("gpsimd", pa[:, :], bass.AP(gall_dr.tensor, 0, [[npp, 128], [1, npp]]))
        paf = p.fence("gpsimd")
        lo = s1.enter_context(nc.sbuf_tensor(nm + "_lo", [1, 8], F32))
        hi = s1.enter_context(nc.sbuf_tensor(nm + "_hi", [1, 8], F32))
        mid = s1.enter_context(nc.sbuf_tensor(nm + "_mid", [1, 8], F32))
        cnt8 = s1.enter_context(nc.sbuf_tensor(nm + "_c8", [128, 8], F32))
        cntr = s1.enter_context(nc.sbuf_tensor(nm + "_cr", [1, 8], F32))
        cond = s1.enter_context(nc.sbuf_tensor(nm + "_cd", [1, 8], U8))
        cmpt = s1.enter_context(nc.sbuf_tensor(nm + "_cmp", [128, 256], F32))
        midb = s1.enter_context(nc.sbuf_tensor(nm + "_mb", [128, 8], F32))
        taub = s1.enter_context(nc.sbuf_tensor(nm + "_tb", [128, 8], F32))
        onc = s1.enter_context(nc.sbuf_tensor(nm + "_onc", [128, 1], F32))
        onb = s1.enter_context(nc.sbuf_tensor(nm + "_onb", [1, 128], BF16))
        ps_c = psum_rot(nc, s1, nm + "_pc", 2, [1, 8])
        ps_m = psum_rot(nc, s1, nm + "_pm", 2, [128, TC])
        hi8 = s1.enter_context(nc.sbuf_tensor(nm + "_h8", [1, 8], BF16))
        lo8f = s1.enter_context(nc.sbuf_tensor(nm + "_l8f", [1, 8], F32))
        lo8 = s1.enter_context(nc.sbuf_tensor(nm + "_l8", [1, 8], BF16))

        def bcast8(row, dst, wtoks, tag):
            h1 = p.v(lambda e: e.tensor_copy(hi8[:, :], row[:, :]), wait=wtoks,
                     inc="v")
            h2 = p.v(lambda e: e.tensor_sub(lo8f[:, :], row[:, :], hi8[:, :]),
                     wait=[h1], inc="v")
            h3 = p.v(lambda e: e.tensor_copy(lo8[:, :], lo8f[:, :]), wait=[h2],
                     inc="v")
            pi_, ps_, pd_ = ps_m.next()
            p.t(lambda e, ps=ps_: e.matmul(ps[0:128, 0:8], onb[:, :], hi8[:, :],
                                           start=True, stop=False),
                wait=pd_ + [h3], inc=None)
            mt_ = p.t(lambda e, ps=ps_: e.matmul(ps[0:128, 0:8], onb[:, :],
                                                 lo8[:, :], start=False, stop=True),
                      inc="t")
            ev_ = p.v(lambda e, ps=ps_: e.tensor_copy(dst[:, :], ps[0:128, 0:8]),
                      wait=[mt_], inc="v")
            ps_m.add_dep(pi_, ev_)
            return [ev_]

        z1 = p.v(lambda e: e.memset(lo[:, :], 0.0), wait=[paf], inc="v")
        z2 = p.v(lambda e: e.memset(hi[:, :], 1.0), wait=[z1], inc="v")
        z3 = p.v(lambda e: e.memset(onc[:, :], 1.0), wait=[z2], inc="v")
        z4 = p.v(lambda e: e.memset(onb[:, :], 1.0), wait=[z3], inc="v")
        prev = p.v(lambda e: e.engine_nop(), wait=[z4], inc="v")
        ntok = prev
        for it in range(30):
            m1 = p.v(lambda e: e.tensor_add(mid[:, :], lo[:, :], hi[:, :]),
                     wait=[ntok], inc="v")
            m2 = p.v(lambda e: e.tensor_scalar_mul(mid[:, :], mid[:, :], 0.5),
                     wait=[m1], inc="v")
            bt = bcast8(mid, midb, [m2], it)
            etoks = []
            for e2 in range(E):
                pav = bass.AP(pa, e2, [[npp, 128], [8, npp // 8]])
                c1_ = p.v(lambda e, o=cmpt, i=pav, tb=midb[:, e2:e2 + 1]:
                          e.tensor_single_scalar(o[:, 0:npp // 8], i, tb, OP.is_ge),
                          wait=bt + (etoks[-1:] if etoks else []), inc="v")
                c2_ = p.v(lambda e, o=cnt8[:, e2:e2 + 1], i=cmpt:
                          e.tensor_reduce(o, i[:, 0:npp // 8], axis=AX.X, op=OP.add),
                          wait=[c1_], inc="v")
                etoks.append(c2_)
            pi, psc, pdeps = ps_c.next()
            mm = p.t(lambda e, ps=psc: e.matmul(ps[:, :], onc[:, :], cnt8[:, :],
                                                start=True, stop=True),
                     wait=pdeps + etoks + [z3], inc="t")
            ev = p.v(lambda e, ps=psc: e.tensor_copy(cntr[:, :], ps[:, :]),
                     wait=[mm], inc="v")
            ps_c.add_dep(pi, ev)
            cd = p.v(lambda e: e.tensor_single_scalar(cond[:, :], cntr[:, :],
                                                      float(cap), OP.is_ge),
                     wait=[ev], inc="v")
            cdb = p.v(lambda e: e.engine_nop(), wait=[cd], inc="v")
            s1_ = p.v(lambda e: e.select(lo[:, :], cond[:, :], mid[:, :], lo[:, :]),
                      wait=[cdb], inc="v")
            s1b = p.v(lambda e: e.engine_nop(), wait=[s1_], inc="v")
            s2_ = p.v(lambda e: e.select(hi[:, :], cond[:, :], hi[:, :], mid[:, :]),
                      wait=[s1b], inc="v")
            ntok = p.v(lambda e: e.engine_nop(), wait=[s2_], inc="v")
        btt = bcast8(lo, taub, [ntok], 99)
        m8p = sbuf_rot(nc, s1, nm + "_m8", 4, [128, 8], U8)
        selp = sbuf_rot(nc, s1, nm + "_sel", 4, [128, 8], F32)
        ixp = sbuf_rot(nc, s1, nm + "_ix", 4, [128, 8], U32)
        ixf = sbuf_rot(nc, s1, nm + "_if", 4, [128, 4], F32)
        asn = sbuf_rot(nc, s1, nm + "_as", 4, [128, 1], U8)
        negt = s1.enter_context(nc.sbuf_tensor(nm + "_ng", [128, 8], F32))
        pst = Rot([s1.enter_context(nc.psum_tensor(f"{nm}_pt{i}", [8, 128], BF16))
                   for i in range(2)])
        gst = sbuf_rot(nc, s1, nm + "_gs", 4, [8, 128], BF16)
        ng1 = p.v(lambda e: e.memset(negt[:, :], -1e29), wait=btt, inc="v")
        ng2 = p.v(lambda e: e.engine_nop(), wait=[ng1], inc="v")
        for j in range(32):
            prj = probs[:, j * 8:(j + 1) * 8]
            bi8, m8t, m8deps = m8p.next()
            t1 = p.v(lambda e, o=m8t, i=prj: e.tensor_tensor(o[:, :], i, taub[:, :],
                                                             OP.is_ge),
                     wait=m8deps + [ng2] + btt, inc="v")
            si, selt, sdeps = selp.next()
            t2 = p.v(lambda e, o=selt, m=m8t, i=prj:
                     e.select(o[:, :], m[:, :], i, negt[:, :]),
                     wait=sdeps + [t1], inc="v")
            t2b = p.v(lambda e: e.engine_nop(), wait=[t2], inc="v")
            mi, mxt, mdeps = mx.next()
            ii, ixt, ideps = ixp.next()
            t3a = p.v(lambda e, m=mxt, i=selt: e.max(out=m[:, :], in_=i[:, :]),
                      wait=mdeps + ideps + [t2b], inc="v")
            t3 = p.v(lambda e, m=mxt, ix=ixt, i=selt:
                     e.max_index(ix[:, :], m[:, :], i[:, :]),
                     wait=[t3a], inc="v")
            t3b = p.v(lambda e: e.engine_nop(), wait=[t3], inc="v")
            fi, fxt, fdeps = ixf.next()
            t4 = p.v(lambda e, o=fxt, i=ixt: e.tensor_copy(o[:, 0:1], i[:, 0:1]),
                     wait=fdeps + [t3b], inc="v")
            ai, ast, adeps = asn.next()
            t5 = p.v(lambda e, o=ast, m=mxt: e.tensor_single_scalar(
                o[:, :], m[:, 0:1], -1e28, OP.is_gt), wait=adeps + [t4], inc="v")
            i2, ixt2, ideps2 = ixp.next()
            m2i, mxt2, mdeps2 = mx.next()
            t6a = p.v(lambda e, m=mxt2, i=prj: e.max(out=m[:, :], in_=i),
                      wait=mdeps2 + ideps2 + [t5], inc="v")
            t6 = p.v(lambda e, m=mxt2, ix=ixt2, i=prj:
                     e.max_index(ix[:, :], m[:, :], i),
                     wait=[t6a], inc="v")
            t6b = p.v(lambda e: e.engine_nop(), wait=[t6], inc="v")
            t7 = p.v(lambda e, o=fxt, i=ixt2: e.tensor_copy(o[:, 1:2], i[:, 0:1]),
                     wait=[t6b], inc="v")
            t8 = p.v(lambda e, o=fxt, a=ast: e.select(o[:, 2:3], a[:, :],
                                                      o[:, 0:1], o[:, 1:2]),
                     wait=[t7], inc="v")
            t8b = p.v(lambda e: e.engine_nop(), wait=[t8], inc="v")
            mx.add_dep(mi, t8)
            mx.add_dep(m2i, t8)
            ixp.add_dep(ii, t8)
            ixp.add_dep(i2, t8)
            asn.add_dep(ai, t8)
            gtoks = []
            for e2 in range(E):
                ge_ = p.v(lambda e, o=selt[:, e2:e2 + 1], i=fxt, e2=e2:
                          e.tensor_single_scalar(o, i[:, 2:3], float(e2),
                                                 OP.is_equal),
                          wait=[t8b] if e2 == 0 else [gtoks[-1]], inc="v")
                gm_ = p.v(lambda e, o=gt[:, j * 8 + e2:j * 8 + e2 + 1],
                          a=selt[:, e2:e2 + 1], b=prj[0:128, e2:e2 + 1]:
                          e.tensor_mul(o, a, b), wait=[ge_], inc="v")
                gtoks.append(gm_)
            selp.add_dep(si, gtoks[-1])
            ixf.add_dep(fi, gtoks[-1])
            pi2, pt_, pdeps2 = pst.next()
            tt = p.t(lambda e, o=pt_, i=gt[:, j * 8:(j + 1) * 8]:
                     e.transpose(o[:, :], i, idb[:, :]),
                     wait=pdeps2 + gtoks, inc="t")
            gi, gs_, gdeps = gst.next()
            ge = p.v(lambda e, o=gs_, i=pt_: e.tensor_copy(o[:, :], i[:, :]),
                     wait=gdeps + [tt], inc="v")
            pst.add_dep(pi2, ge)
            p.dma("gpsimd", grow_dr[:, j * 128:(j + 1) * 128], gs_[:, :],
                  wait=[ge])
            gst.add_dep(gi, ("v", p.cnt["v"]))
        gf = p.fence("gpsimd")
    p.set_ready([gf])

    xp_tiles = []
    for kk in range(NDT):
        tile = st.enter_context(nc.sbuf_tensor(f"{nm}_xpt{kk}", [128, T], BF16))
        p.dma("sync", tile[:, :], xp_bf[kk * 128:(kk + 1) * 128, :])
        xp_tiles.append(tile)
    xpf = p.fence("sync")
    cache_xp = {id(xp_bf.tensor): xp_tiles}
    prev_toks = [xpf]
    for e in range(E):
        res_in = x_dram if e == 0 else acc[(e + 1) % 2]
        out_dr = xnew_dram if e == E - 1 else acc[e % 2]
        with ExitStack() as se:
            ones_rb = se.enter_context(nc.sbuf_tensor(f"{nm}_ob{e}", [1, 128], BF16))
            grow_sb = se.enter_context(nc.sbuf_tensor(f"{nm}_grs{e}", [1, T], BF16))
            o1 = p.v(lambda e2: e2.memset(ones_rb[:, :], 1.0), wait=prev_toks, inc="v")
            o2 = p.v(lambda e2: e2.engine_nop(), wait=[o1], inc="v")
            p.dma("scalar", grow_sb[:, :], grow_dr[e:e + 1, :], wait=prev_toks)
            grf = p.fence("scalar")
            psb = psum_rot(nc, se, f"{nm}_pb{e}", 2, [128, TC])
            gateb = []
            for tc in range(NTC):
                pi, ps, pdeps = psb.next()
                bm = p.t(lambda e2, ps=ps, r=grow_sb[0:1, tc * TC:(tc + 1) * TC]:
                         e2.matmul(ps[:, :], ones_rb[:, :], r, start=True, stop=True),
                         wait=pdeps + [o2, grf], inc="t")
                gtile = se.enter_context(
                    nc.sbuf_tensor(f"{nm}_gb{e}_{tc}", [128, TC], BF16))
                ev = p.v(lambda e2, o=gtile, ps=ps: e2.tensor_copy(o[:, :], ps[:, :]),
                         wait=[bm], inc="v")
                psb.add_dep(pi, ev)
                gateb.append((gtile[:, :], ev))
            th = emit_ffn_h(p, se, f"{nm}_h{e}", h_dr,
                            W["W1"][e * D:(e + 1) * D, :], xp_bf, AF.Gelu, F,
                            gateb=gateb, rhs_cache=cache_xp)
            p.set_ready(th)
        with ExitStack() as se2:
            td = emit_mm_bigK(p, se2, f"{nm}_d{e}", [(out_dr, F32)],
                              [(W["W2"][e * F:(e + 1) * F, :], h_dr, F)],
                              D, T, res_dram=(res_in,))
        p.set_ready(td)
        prev_toks = td
    return prev_toks



def emit_fattn(p, st, nm, x_dram, xnew_dram, W, scr_dram):
    nc = p.nc
    H, hd = 16, 64
    NKT = T // 128
    xp_bf = scr_dram(nm + "_xp", [D, T], BF16)
    Q_dr = scr_dram(nm + "_Q", [D, T], BF16)
    K_dr = scr_dram(nm + "_K", [D, T], BF16)
    V_dr = scr_dram(nm + "_V", [T, D], BF16)
    O_dr = scr_dram(nm + "_O", [D, T], BF16)
    Wv_bf = scr_dram(nm + "_wvb", [D, D], BF16)
    bv_bf = scr_dram(nm + "_bvb", [1, D], BF16)
    tps_dr = scr_dram(nm + "_tps", [16, 8191], BF16)
    bm_dr = scr_dram(nm + "_bmr", [16, 2], F32)
    qmk_dr = scr_dram(nm + "_qmr", [2, 16], F32)

    with ExitStack() as s0:
        t0 = emit_ln(p, s0, nm + "_ln", x_dram, (), [(xp_bf, BF16)])
    p.set_ready(t0)

    s_pp = st.enter_context(nc.sbuf_tensor(nm + "_spp", [128, 1], F32))
    srow = st.enter_context(nc.sbuf_tensor(nm + "_srw", [1, 4], F32))
    sh8 = st.enter_context(nc.sbuf_tensor(nm + "_sh8", [1, 1], BF16))
    sl8f = st.enter_context(nc.sbuf_tensor(nm + "_sl8f", [1, 1], F32))
    sl8 = st.enter_context(nc.sbuf_tensor(nm + "_sl8", [1, 1], BF16))
    onbг = st.enter_context(nc.sbuf_tensor(nm + "_onb", [1, 128], BF16))
    pre_ps = ExitStack()
    ps_small = psum_rot(nc, pre_ps, nm + "_pss", 2, [128, TC])
    p.dma("scalar", srow[:, 0:1], W["temperature"][0:1, 0:1])
    tf_ = p.fence("scalar")
    u1 = p.v(lambda e: e.tensor_scalar_max(srow[:, 1:2], srow[:, 0:1], 0.1),
             wait=[tf_], inc="v")
    u2 = p.v(lambda e: e.tensor_scalar_min(srow[:, 1:2], srow[:, 1:2], 10.0),
             wait=[u1], inc="v")
    u3 = p.v(lambda e: e.tensor_scalar_mul(srow[:, 1:2], srow[:, 1:2], 32.0),
             wait=[u2], inc="v")
    u4 = p.v(lambda e: e.reciprocal(srow[:, 2:3], srow[:, 1:2]), wait=[u3], inc="v")
    u5 = p.v(lambda e: e.engine_nop(), wait=[u4], inc="v")
    ob = p.v(lambda e: e.memset(onbг[:, :], 1.0), wait=[u5], inc="v")
    ob2 = p.v(lambda e: e.engine_nop(), wait=[ob], inc="v")
    h1 = p.v(lambda e: e.tensor_copy(sh8[:, :], srow[:, 2:3]), wait=[ob2], inc="v")
    h2 = p.v(lambda e: e.tensor_sub(sl8f[:, :], srow[:, 2:3], sh8[:, :]),
             wait=[h1], inc="v")
    h3 = p.v(lambda e: e.tensor_copy(sl8[:, :], sl8f[:, :]), wait=[h2], inc="v")
    pi_, ps_, pd_ = ps_small.next()
    p.t(lambda e, ps=ps_: e.matmul(ps[0:128, 0:1], onbг[:, :], sh8[:, :],
                                   start=True, stop=False), wait=pd_ + [h3])
    smt = p.t(lambda e, ps=ps_: e.matmul(ps[0:128, 0:1], onbг[:, :], sl8[:, :],
                                         start=False, stop=True), inc="t")
    sev = p.v(lambda e, ps=ps_: e.tensor_copy(s_pp[:, :], ps[0:128, 0:1]),
              wait=[smt], inc="v")
    ps_small.add_dep(pi_, sev)

    with ExitStack() as s0:
        cv = emit_cast_dram(p, s0, nm + "_cw", W["Wv"], Wv_bf, D, D)
        cb = emit_cast_dram(p, s0, nm + "_cb", W["bv"], bv_bf, 1, D)
    p.set_ready(cv + cb)
    with ExitStack() as s0:
        cache = {}
        tq = emit_mm(p, s0, nm + "_q", [(Q_dr, BF16)],
                     [(W["Wq"], F32, xp_bf, D), (W["bq"], F32, None, 1)], D, T,
                     scale_pp=lambda mt: (s_pp[:, 0:1], sev), rhs_cache=cache)
        tk = emit_mm(p, s0, nm + "_k", [(K_dr, BF16)],
                     [(W["Wk"], F32, xp_bf, D), (W["bk"], F32, None, 1)], D, T,
                     rhs_cache=cache)
        tv = emit_mm(p, s0, nm + "_v", [(V_dr, BF16)],
                     [(xp_bf, BF16, Wv_bf, D), (W["ones_row"], F32, bv_bf, 1)],
                     T, D)
    p.set_ready(tq + tk + tv)

    with ExitStack() as s0:
        tp_f = s0.enter_context(nc.sbuf_tensor(nm + "_tpf", [16, 8191], F32))
        tp_b = s0.enter_context(nc.sbuf_tensor(nm + "_tpb", [16, 8191], BF16))
        bm16 = s0.enter_context(nc.sbuf_tensor(nm + "_bm", [16, 2], F32))
        p.dma("sync", tp_f[:, :], W["t_pad"][:, :])
        tpf_ = p.fence("sync")
        sc1 = p.v(lambda e: e.tensor_scalar_mul(tp_f[:, :], tp_f[:, :],
                                                s_pp[0:16, 0:1]),
                  wait=[tpf_, sev], inc="v")
        sc2 = p.v(lambda e: e.tensor_copy(tp_b[:, :], tp_f[:, :]), wait=[sc1],
                  inc="v")
        p.dma("gpsimd", tps_dr[:, :], tp_b[:, :], wait=[sc2])
        sc3 = p.v(lambda e: e.tensor_reduce(bm16[:, 0:1], tp_f[:, :], axis=AX.X,
                                            op=OP.max, apply_absolute_value=True),
                  wait=[sc1], inc="v")
        sc4 = p.v(lambda e: e.tensor_copy(bm16[:, 1:2], tp_f[:, 1998:1999]),
                  wait=[sc3], inc="v")
        p.dma("gpsimd", bm_dr[:, :], bm16[:, :], wait=[sc4])
        tpsf = p.fence("gpsimd")
    p.set_ready([tpsf])

    with ExitStack() as s0:
        ind = s0.enter_context(nc.sbuf_tensor(nm + "_ind", [128, 2], F32))
        i1 = p.v(lambda e: e.memset(ind[:, :], 0.0), inc="v")
        i2 = p.v(lambda e: e.memset(ind[0:64, 0:1], 1.0), wait=[i1], inc="v")
        i3 = p.v(lambda e: e.memset(ind[64:128, 1:2], 1.0), wait=[i2], inc="v")
        i4 = p.v(lambda e: e.engine_nop(), wait=[i3], inc="v")
        qmk = s0.enter_context(nc.sbuf_tensor(nm + "_qmk", [2, 2 * NDT], F32))
        z = p.v(lambda e: e.memset(qmk[:, :], 0.0), wait=[i4], inc="v")
        zz = p.v(lambda e: e.engine_nop(), wait=[z], inc="v")
        xb = sbuf_rot(nc, s0, nm + "_xb", 4, [128, TC], BF16)
        sq = sbuf_rot(nc, s0, nm + "_sq", 2, [128, TC], F32)
        ps_n = psum_rot(nc, s0, nm + "_pn", 2, [2, TC])
        rmx = sbuf_rot(nc, s0, nm + "_rx", 2, [2, 1], F32)
        prevv = zz
        for src_i, src_dr in enumerate((Q_dr, K_dr)):
            for dt in range(NDT):
                for tc in range(NTC):
                    xi, xt, xdeps = xb.next()
                    p.dma("sync", xt[:, :],
                          src_dr[dt * 128:(dt + 1) * 128, tc * TC:(tc + 1) * TC],
                          wait=xdeps)
                    xf = p.fence("sync")
                    qi, qt, qdeps = sq.next()
                    sqt = p.s(lambda e, o=qt, i=xt: e.activation(o[:, :], i[:, :],
                                                                 AF.Square),
                              wait=qdeps + [xf], inc="a")
                    xb.add_dep(xi, sqt)
                    pi, psn, pdeps = ps_n.next()
                    mmn = p.t(lambda e, ps=psn, q=qt: e.matmul(
                        ps[:, :], ind[:, :], q[:, :], start=True, stop=True),
                        wait=pdeps + [sqt, i4], inc="t")
                    sq.add_dep(qi, mmn)
                    ri, rmt, rdeps = rmx.next()
                    r1 = p.v(lambda e, o=rmt, ps=psn: e.tensor_reduce(
                        o[:, :], ps[:, :], axis=AX.X, op=OP.max),
                        wait=rdeps + [mmn], inc="v")
                    ps_n.add_dep(pi, r1)
                    col = src_i * NDT + dt
                    r2 = p.v(lambda e, o=qmk[:, col:col + 1], i=rmt:
                             e.tensor_max(o, o, i[:, :]),
                             wait=[r1, prevv], inc="v")
                    rmx.add_dep(ri, r2)
                    prevv = r2
        p.dma("gpsimd", qmk_dr[:, :], qmk[:, :], wait=[prevv])
        nf = p.fence("gpsimd")
    p.set_ready([nf])

    cb_const = []
    cb_band = []
    with ExitStack() as s0:
        rws = st.enter_context(nc.sbuf_tensor(nm + "_rw2", [1, 4 * 16], F32))
        crow = st.enter_context(nc.sbuf_tensor(nm + "_crw", [1, 64], F32))
        crb = st.enter_context(nc.sbuf_tensor(nm + "_crb", [1, 16 * 2], BF16))
        p.dma("scalar", rws[:, 0:32], bass.AP(bm_dr.tensor, 0, [[32, 1], [1, 32]]))
        p.dma("scalar", rws[:, 32:64],
              bass.AP(qmk_dr.tensor, 0, [[32, 1], [1, 32]]))
        rf = p.fence("scalar")
        c1a = p.v(lambda e: e.tensor_mul(crow[:, 0:8], rws[:, 32:40],
                                         rws[:, 40:48]), wait=[rf], inc="v")
        c1b = p.v(lambda e: e.tensor_mul(crow[:, 8:16], rws[:, 48:56],
                                         rws[:, 56:64]), wait=[c1a], inc="v")
        c2 = p.s(lambda e: e.activation(crow[:, 0:16], crow[:, 0:16], AF.Sqrt),
                 wait=[c1b], inc="a")
        ctoks = [c2]
        for h2 in range(16):
            pos2 = (h2 % 2) * 8 + h2 // 2
            ca = p.v(lambda e, h2=h2, pos2=pos2: e.tensor_add(
                crow[:, 32 + h2:33 + h2], crow[:, pos2:pos2 + 1],
                rws[:, 2 * h2:2 * h2 + 1]), wait=[ctoks[-1]], inc="v")
            ctoks.append(ca)
        c4 = p.v(lambda e: e.tensor_scalar_mul(crow[:, 48:64], crow[:, 32:48],
                                               -1.0), wait=[ctoks[-1]], inc="v")
        c0v = st.enter_context(nc.sbuf_tensor(nm + "_c0v", [1, 16], F32))
        c5a = p.v(lambda e: e.tensor_copy(
            c0v[:, :], bass.AP(rws, 1, [[64, 1], [2, 16]])), wait=[c4], inc="v")
        c5 = p.v(lambda e: e.tensor_sub(crow[:, 32:48], c0v[:, :],
                                        crow[:, 32:48]), wait=[c5a], inc="v")
        c6a = p.v(lambda e: e.tensor_copy(crb[:, 0:16], crow[:, 32:48]),
                  wait=[c5], inc="v")
        c6 = p.v(lambda e: e.tensor_copy(crb[:, 16:32], crow[:, 48:64]),
                 wait=[c6a], inc="v")
        for h in range(H):
            for kind in range(2):
                tile = st.enter_context(
                    nc.sbuf_tensor(f"{nm}_cb{h}_{kind}", [128, 1], F32))
                pos = kind * 16 + h
                pi, ps2, pd2 = ps_small.next()
                bmm = p.t(lambda e, ps=ps2, r=crb[0:1, pos:pos + 1]:
                          e.matmul(ps[0:128, 0:1], onbг[:, :], r,
                                   start=True, stop=True),
                          wait=pd2 + [c6, ob2], inc="t")
                bev = p.v(lambda e, o=tile, ps=ps2:
                          e.tensor_copy(o[:, :], ps[0:128, 0:1]),
                          wait=[bmm], inc="v")
                ps_small.add_dep(pi, bev)
                (cb_const if kind == 0 else cb_band).append((tile, bev))
    pre_ps.close()

    idnJ = np.zeros((128, 128), dtype=np.float32)
    for i in range(128):
        idnJ[i, 127 - i] = 1.0
    Jt = nc.inline_tensor(idnJ, name=nm + "_J").ap()

    with ExitStack() as s1:
        jb = s1.enter_context(nc.sbuf_tensor(nm + "_jb", [128, 128], BF16))
        jf = s1.enter_context(nc.sbuf_tensor(nm + "_jf", [128, 128], F32))
        p.dma("sync", jf[:, :], Jt)
        jfc = p.fence("sync")
        jt = p.v(lambda e: e.tensor_copy(jb[:, :], jf[:, :]), wait=[jfc], inc="v")
        kres = s1.enter_context(nc.sbuf_tensor(nm + "_kres", [64, T], BF16))
        vres = s1.enter_context(nc.sbuf_tensor(nm + "_vres", [128, 65 * NKT], BF16))
        qch = sbuf_rot(nc, s1, nm + "_qch", 2, [64, TC], BF16)
        hkb = sbuf_rot(nc, s1, nm + "_hk", 3, [128, TC], BF16)
        ps_sT = psum_rot(nc, s1, nm + "_pt", 3, [128, TC])
        ptile = sbuf_rot(nc, s1, nm + "_pb", NKT, [128, TC], BF16)
        ps_o = psum_rot(nc, s1, nm + "_po", 2, [65, TC])
        ps_rb = psum_rot(nc, s1, nm + "_prb", 2, [64, TC])
        sumr = sbuf_rot(nc, s1, nm + "_sum", 2, [1, 3 * TC], F32)
        shl = sbuf_rot(nc, s1, nm + "_shl", 2, [1, 2 * TC], BF16)
        ost = sbuf_rot(nc, s1, nm + "_ost", 2, [64, TC], F32)
        obf = sbuf_rot(nc, s1, nm + "_obf", 4, [64, TC], BF16)
        prev_h = [jt, cb_band[-1][1], cb_const[-1][1]]
        for h in range(H):
            kl = p.dma("sync", kres[:, :], K_dr[h * 64:(h + 1) * 64, :],
                       wait=prev_h)
            vz = p.v(lambda e: e.memset(vres[:, :], 1.0), wait=prev_h, inc="v")
            vz2 = p.v(lambda e: e.engine_nop(), wait=[vz], inc="v")
            for kt in range(NKT):
                p.dma("sync", vres[:, kt * 65:kt * 65 + 64],
                      V_dr[kt * 128:(kt + 1) * 128, h * 64:(h + 1) * 64],
                      wait=[vz2])
            kvf = p.fence("sync")
            ostoks = []
            for qc in range(NTC):
                qi, qt_, qdeps = qch.next()
                p.dma("sync", qt_[:, :],
                      Q_dr[h * 64:(h + 1) * 64, qc * TC:(qc + 1) * TC],
                      wait=qdeps)
                qf = p.fence("sync")
                ptoks = []
                for kt in range(NKT):
                    band = (kt * 128 + 127 - qc * TC) > 2097
                    pi, psT, pdeps = ps_sT.next()
                    m1 = p.t(lambda e, ps=psT, k=kres[:, kt * 128:(kt + 1) * 128],
                             q=qt_: e.matmul(ps[:, :], k, q[:, :],
                                             start=True, stop=not band),
                             wait=pdeps + [kvf, qf], inc=None if band else "t")
                    if band:
                        hbase = qc * TC - (kt * 128 + 127) + 4095
                        hi2, hkt, hdeps = hkb.next()
                        p.dma("scalar", hkt[:, :],
                              bass.AP(tps_dr.tensor, h * 8191 + hbase,
                                      [[1, 128], [1, TC]]), wait=hdeps)
                        hf2 = p.fence("scalar")
                        m1 = p.t(lambda e, ps=psT, hk=hkt:
                                 e.matmul(ps[:, :], jb[:, :], hk[:, :],
                                          start=False, stop=True),
                                 wait=[hf2, jt], inc="t")
                        hkb.add_dep(hi2, m1)
                    cbt, cbtok = (cb_band if band else cb_const)[h]
                    pti, ptt, ptdeps = ptile.next()
                    ev = p.s(lambda e, o=ptt, ps=psT, b=cbt:
                             e.activation(o[:, :], ps[:, :], AF.Exp,
                                          bias=b[:, 0:1]),
                             wait=ptdeps + [m1, cbtok], inc="a")
                    ps_sT.add_dep(pi, ev)
                    ptoks.append((pti, ev))
                oi, pso, odeps = ps_o.next()
                omm = None
                for kt in range(NKT):
                    omm = p.t(lambda e, ps=pso,
                              v=vres[:, kt * 65:(kt + 1) * 65],
                              r=ptile.tiles[ptoks[kt][0]][:, :],
                              f=(kt == 0), l=(kt == NKT - 1):
                              e.matmul(ps[:, :], v, r[:, :], start=f, stop=l),
                              wait=(odeps if kt == 0 else []) + [ptoks[kt][1]],
                              inc="t" if kt == NKT - 1 else None)
                for (pti, ev) in ptoks:
                    ptile.add_dep(pti, omm)
                si, sr, sdeps = sumr.next()
                s1_ = p.v(lambda e, o=sr, ps=pso: e.tensor_copy(
                    o[0:1, 0:TC], ps[64:65, :]), wait=sdeps + [omm], inc="v")
                s2_ = p.s(lambda e, o=sr: e.activation(o[0:1, TC:2 * TC],
                                                       o[0:1, 0:TC], AF.Ln),
                          wait=[s1_], inc="a")
                negap, negtok = p.constp(-1.0)
                s3_ = p.s(lambda e, o=sr, na=negap: e.activation(
                    o[0:1, 2 * TC:3 * TC], o[0:1, TC:2 * TC], AF.Exp,
                    scale=na[0:1, 0:1]), wait=[s2_, negtok], inc="a")
                hi3, sh, shdeps = shl.next()
                b1 = p.v(lambda e, o=sh, i=sr: e.tensor_copy(
                    o[0:1, 0:TC], i[0:1, 2 * TC:3 * TC]),
                    wait=shdeps + [s3_], inc="v")
                b2 = p.v(lambda e, o=sr, i=sh: e.tensor_sub(
                    o[0:1, 0:TC], o[0:1, 2 * TC:3 * TC], i[0:1, 0:TC]),
                    wait=[b1], inc="v")
                b3 = p.v(lambda e, o=sh, i=sr: e.tensor_copy(
                    o[0:1, TC:2 * TC], i[0:1, 0:TC]), wait=[b2], inc="v")
                ri2, psrb, rbdeps = ps_rb.next()
                p.t(lambda e, ps=psrb, r=sh: e.matmul(
                    ps[:, :], onbг[0:1, 0:64], r[0:1, 0:TC],
                    start=True, stop=False), wait=rbdeps + [b3, ob2])
                rmm = p.t(lambda e, ps=psrb, r=sh: e.matmul(
                    ps[:, :], onbг[0:1, 0:64], r[0:1, TC:2 * TC],
                    start=False, stop=True), inc="t")
                shl.add_dep(hi3, rmm)
                sumr.add_dep(si, rmm)
                oi2, ot_, osdeps = ost.next()
                oe = p.s(lambda e, o=ot_, ps=pso: e.copy(o[:, :], ps[0:64, :]),
                         wait=osdeps + [omm], inc="a")
                ps_o.add_dep(oi, oe)
                bi2, obt, obdeps = obf.next()
                om = p.v(lambda e, o=obt, a=ot_, ps=psrb:
                         e.tensor_mul(o[:, :], a[:, :], ps[:, :]),
                         wait=obdeps + [oe, rmm], inc="v")
                ost.add_dep(oi2, om)
                ps_rb.add_dep(ri2, om)
                p.dma("gpsimd", O_dr[h * 64:(h + 1) * 64,
                                     qc * TC:(qc + 1) * TC], obt[:, :],
                      wait=[om])
                ostoks.append(bi2)
                qch.add_dep(qi, omm)
            gfh = p.fence("gpsimd")
            for bi2 in ostoks:
                obf.add_dep(bi2, gfh)
            prev_h = [gfh, ("t", p.cnt["t"])]
    p.set_ready([gfh])

    with ExitStack() as s3:
        toks = emit_mm(p, s3, nm + "_o", [(xnew_dram, F32)],
                       [(W["Wp"], F32, O_dr, D), (W["bp"], F32, None, 1)],
                       D, T, res_dram=(x_dram,))
    p.set_ready(toks)
    return toks



def _build_sub(kind, wspecs, n_cores=8):
    nc = bass.Bass("TRN2", target_bir_lowering=False)
    x = nc.dram_tensor("x", [D, T], F32, kind="ExternalInput").ap()
    xn = nc.dram_tensor("xn", [D, T], F32, kind="ExternalOutput").ap()
    wdr = {}
    for nm2, shape in wspecs:
        wdr[nm2] = nc.dram_tensor(nm2, list(shape), F32, kind="ExternalInput").ap()

    def scr_dram(name, shape, dtype):
        return nc.dram_tensor(name, shape, dtype, kind="Internal").ap()

    with ExitStack() as stack:
        p = P(nc, stack)
        emitters = dict(rwkv=emit_rwkv, ffn0=emit_ffn0, ttt=emit_ttt,
                        wattn=emit_wattn, moe=emit_moe, fattn=emit_fattn)
        if kind == "ffn0":
            toks = emit_ffn0(p, stack, "s", x, xn, wdr, scr_dram, n_cores=n_cores)
        else:
            toks = emitters[kind](p, stack, "s", x, xn, wdr, scr_dram)
        p.g(lambda e: e.engine_nop(), wait=toks, inc="p")
        p.build(stack)
    return nc


_PROG_CACHE = {}


def _get_prog(kind, wspecs):
    if kind not in _PROG_CACHE:
        _PROG_CACHE[kind] = _build_sub(kind, wspecs)
    return _PROG_CACHE[kind]


def kernel(**inp):
    c = np.ascontiguousarray
    B = 8
    x0 = inp["L0_attn_a_x"]
    xs = [c(x0[b].T.astype(np.float32)) for b in range(B)]
    ones_row = np.ones((1, T), np.float32)
    pos_row = (1.0 / (1.0 + 0.1 * np.log(np.arange(1, T + 1, dtype=np.float64)))
               ).astype(np.float32)[None]
    bt = inp["L1_attn_b_bias_table"]
    tp = np.concatenate([bt, np.repeat(bt[1998:1999], 8191 - 1999, axis=0)],
                        axis=0)
    t_pad = c(tp.T)

    subws = {
        "rwkv": {
            "WkT": c(inp["L0_attn_a_W_k"].T), "WvT": c(inp["L0_attn_a_W_v"].T),
            "WrT": c(inp["L0_attn_a_W_r"].T), "WoT": c(inp["L0_attn_a_W_o"].T),
            "tmk": c(inp["L0_attn_a_time_mix_k"].reshape(-1, 1)),
            "tmv": c(inp["L0_attn_a_time_mix_v"].reshape(-1, 1)),
            "tmr": c(inp["L0_attn_a_time_mix_r"].reshape(-1, 1)),
            "decay": c(inp["L0_attn_a_time_decay"].reshape(-1, 1)),
            "first": c(inp["L0_attn_a_time_first"].reshape(-1, 1)),
        },
        "ffn0": {
            "Wg": c(inp["L0_ffn_Wg"]),
            "W1": c(inp["L0_ffn_W1"].reshape(8 * D, 2048)),
            "W2": c(inp["L0_ffn_W2"].reshape(8 * 2048, D)),
        },
        "ttt": {
            "WqT": c(inp["L0_attn_b_Wq"].T), "WkT": c(inp["L0_attn_b_Wk"].T),
            "WvT": c(inp["L0_attn_b_Wv"].T), "WoT": c(inp["L0_attn_b_Wo"].T),
            "WgT": c(inp["L0_attn_b_Wg"].T), "W0": c(inp["L0_attn_b_W0"]),
            "A": c(inp["L0_attn_b_A_init"]), "Bm": c(inp["L0_attn_b_B_init"]),
            "log_lr": c(inp["L0_attn_b_log_lr"].reshape(-1, 1)),
            "ln_g": c(inp["L0_attn_b_ln_g"].reshape(-1, 1)),
            "ln_b": c(inp["L0_attn_b_ln_b"].reshape(-1, 1)),
            "ls_g": c(inp["L0_attn_b_ls_g"].reshape(-1, 1)),
            "pos_row": pos_row,
        },
        "wattn": {
            "q_wT": c(inp["L1_attn_a_q_w"].T),
            "kv_wT": c(inp["L1_attn_a_kv_w"].T),
            "proj_wT": c(inp["L1_attn_a_proj_w"].T),
            "proj_b": c(inp["L1_attn_a_proj_b"].reshape(1, -1)),
        },
        "moe": {
            "wgT": c(inp["L1_ffn_wg"].T),
            "bg": c(inp["L1_ffn_bg"].reshape(1, -1)),
            "ones_row": ones_row,
            "WgateT": c(inp["L1_ffn_Wgate"].transpose(0, 2, 1).reshape(8 * D, 4096)),
            "WupT": c(inp["L1_ffn_Wup"].transpose(0, 2, 1).reshape(8 * D, 4096)),
            "WdownT": c(inp["L1_ffn_Wdown"].transpose(0, 2, 1).reshape(8 * 4096, D)),
        },
        "fattn": {
            "Wq": c(inp["L1_attn_b_Wq"]), "Wk": c(inp["L1_attn_b_Wk"]),
            "Wv": c(inp["L1_attn_b_Wv"]), "Wp": c(inp["L1_attn_b_Wp"]),
            "bq": c(inp["L1_attn_b_bq"].reshape(1, -1)),
            "bk": c(inp["L1_attn_b_bk"].reshape(1, -1)),
            "bv": c(inp["L1_attn_b_bv"].reshape(1, -1)),
            "bp": c(inp["L1_attn_b_bp"].reshape(1, -1)),
            "t_pad": t_pad,
            "temperature": c(inp["L1_attn_b_temperature"].reshape(1, 1)),
            "ones_row": ones_row,
        },
    }
    seq = ["rwkv", "ffn0", "ttt", "ffn0", "wattn", "moe", "fattn", "moe"]
    for kind in seq:
        ws = subws[kind]
        wspecs = [(k2, v.shape) for k2, v in sorted(ws.items())]
        nc = _get_prog(kind, wspecs)
        in_maps = [dict(ws, x=xs[b]) for b in range(B)]
        res = run_bass_kernel_spmd(nc, in_maps, core_ids=list(range(B)))
        xs = [np.asarray(r["xn"]) for r in res.results]
    return np.stack([x.T for x in xs], axis=0)
